# version 13
# baseline (speedup 1.0000x reference)
import numpy as np
import ml_dtypes
import concourse.bass as bass
import concourse.mybir as mybir
from concourse.bass import IndirectOffsetOnAxis
from concourse.bass_utils import run_bass_kernel_spmd
from contextlib import ExitStack

F32 = mybir.dt.float32
BF16 = mybir.dt.bfloat16
U32 = mybir.dt.uint32
OP = mybir.AluOpType
AF = mybir.ActivationFunctionType
AX = mybir.AxisListType

B, T, C, E, CAP, RH, DFF = 4, 8192, 1024, 16, 256, 128, 2048
TC = 512
NCH = T // TC
SUB = 256
SCN = 512
CT = C // 128
XOF = 16
E60 = float(np.exp(60.0))
EPS_WK = 1e-8
EPS_W = 1e-9
EPS_LN = 1e-5
BIS_ITERS = 36

bfdt = ml_dtypes.bfloat16


def _consts():
    p = np.arange(128)
    grp = p // 8
    return dict(
        ltri=(p[None, :] > p[:, None]).astype(np.float32),
        shift1=(p[None, :] == p[:, None] + 1).astype(np.float32),
        btri8=((grp[None, :] == grp[:, None]) & (p[None, :] > p[:, None])).astype(np.float32),
        sel16to128=(np.arange(16)[:, None] == grp[None, :]).astype(np.float32),
        sel128to16=(grp[:, None] == np.arange(16)[None, :]).astype(np.float32),
        identb=np.eye(128).astype(bfdt),
        identf=np.eye(128).astype(np.float32),
    )


def build_program(upto="full", nchunks=NCH):
    nc = bass.Bass("TRN2", target_bir_lowering=False, debug=False)

    def din(name, shape, dtype=F32):
        return nc.dram_tensor(name, list(shape), dtype, kind="ExternalInput").ap()

    xq = din("xq", [T, C])
    Ws = [din(n, [C, C]) for n in ("Wk", "Wv", "Wr", "Wo")]
    tmk, tmv, tmr = din("tmk", [C]), din("tmv", [C]), din("tmr", [C])
    td, tfi = din("td", [C]), din("tfi", [C])
    rw1, rw2 = din("rw1", [C, RH]), din("rw2", [RH, E])
    w1d, w2d, w3d = din("w1", [E, C, DFF]), din("w2", [E, C, DFF]), din("w3", [E, DFF, C])
    cst = {k: din("cin_" + k, v.shape, mybir.dt.from_np(v.dtype)) for k, v in _consts().items()}

    out = nc.dram_tensor("out", [T, C], F32, kind="ExternalOutput").ap()
    dbg = {}
    if upto in ("x1", "rl", "topk"):
        dbg["x1"] = nc.dram_tensor("dbg_x1", [T, C], F32, kind="ExternalOutput").ap()
    if upto in ("rl", "topk"):
        dbg["rl"] = nc.dram_tensor("dbg_rl", [E, T], F32, kind="ExternalOutput").ap()
    if upto == "topk":
        dbg["tok"] = nc.dram_tensor("dbg_tok", [128, E * 2], F32, kind="ExternalOutput").ap()
        dbg["th"] = nc.dram_tensor("dbg_th", [16, 8], F32, kind="ExternalOutput").ap()

    xbfd = nc.dram_tensor("xbfd", [T, C], BF16).ap()
    wtd = nc.dram_tensor("wtd", [4, C, C], BF16).ap()
    gsrc = nc.dram_tensor("gsrc", [T, C], BF16).ap()
    rlg = nc.dram_tensor("rlg", [T, E], F32).ap()
    rld = nc.dram_tensor("rld", [E, T], F32).ap()
    keyd = nc.dram_tensor("keyd", [E, T], F32).ap()
    idsd = nc.dram_tensor("idsd", [E, T], F32).ap()
    accum = nc.dram_tensor("accum", [T, C], F32).ap()
    eisc = nc.dram_tensor("eisc", [2, CAP, C], BF16).ap()

    streams = {k: [] for k in ("sync", "scalar", "vector", "tensor", "gpsimd")}
    full = (nchunks == NCH)

    with ExitStack() as st:
        def sbt(name, shape, dtype=F32):
            return st.enter_context(nc.sbuf_tensor(name, shape, dtype))

        wkT = sbt("wkT", [128, CT, C], BF16)
        wvT = sbt("wvT", [128, CT, C], BF16)
        wrT = sbt("wrT", [128, CT, C], BF16)
        woT = sbt("woT", [128, CT, C], BF16)
        rw1T = sbt("rw1T", [128, CT, RH], BF16)
        rw2b = sbt("rw2b", [128, E], BF16)
        vtmk, vtmv, vtmr = sbt("vtmk", [128, CT]), sbt("vtmv", [128, CT]), sbt("vtmr", [128, CT])
        vdec, vtf = sbt("vdec", [128, CT]), sbt("vtf", [128, CT])
        ones1 = sbt("ones1", [1, 128])
        c_ltri, c_shift = sbt("c_ltri", [128, 128]), sbt("c_shift", [128, 128])
        c_btri = sbt("c_btri", [128, 128])
        c_s16, c_s128 = sbt("c_s16", [16, 128]), sbt("c_s128", [128, 16])
        c_idf = sbt("c_idf", [128, 128])

        xT = sbt("xT", [128, 2, CT, TC + XOF], BF16)
        xm = sbt("xm", [128, 2, CT, TC], BF16)
        xmB = sbt("xmB", [128, CT, TC], BF16)
        kvb = sbt("kvb", [128, CT, TC])
        kcb = sbt("kcb", [128, CT, TC])
        ybuf = sbt("ybuf", [128, CT, SCN + 1])
        ckv = sbt("ckv", [128, CT])
        ckc = sbt("ckc", [128, CT])
        srb = sbt("srb", [128, CT, TC], BF16)
        oTb = sbt("oTb", [128, CT, TC], BF16)
        xrow = sbt("xrow", [128, 2, C])
        x1p = sbt("x1p", [128, 2, C])
        x1T = sbt("x1T", [128, CT, TC], BF16)
        hidT = sbt("hidT", [128, TC], BF16)
        rlc = sbt("rlc", [16, TC])
        rlt = sbt("rlt", [128, 4, 16])
        lsq, lmn, lrs = sbt("lsq", [128, 8]), sbt("lmn", [128, 8]), sbt("lrs", [128, 8])
        Mrun, mrun = sbt("Mrun", [16, 1]), sbt("mrun", [16, 1])
        t16a, t16b = sbt("t16a", [16, 1]), sbt("t16b", [16, 1])
        lo16, hi16, mid16 = sbt("lo16", [16, 1]), sbt("hi16", [16, 1]), sbt("mid16", [16, 1])
        q16, z16, dn16 = sbt("q16", [16, 1]), sbt("z16", [16, 1]), sbt("dn16", [16, 1])
        cnt16, n116 = sbt("cnt16", [16, 1]), sbt("n116", [16, 1])
        row16 = sbt("row16", [1, 16])
        cntp = sbt("cntp", [128, 1])
        r1, r2, r3, r4 = sbt("r1", [128, 1]), sbt("r2", [128, 1]), sbt("r3", [128, 1]), sbt("r4", [128, 1])
        c_epswk = sbt("c_epswk", [128, 1])
        c_one1 = sbt("c_one1", [128, 1])
        nM128, rd128 = sbt("nM128", [128, 16]), sbt("rd128", [128, 16])
        sc1 = sbt("sc1", [128, 1024])
        keyt = sbt("keyt", [128, 64])
        keytab = sbt("keytab", [128, 64])
        idst = sbt("idst", [128, 64])
        iotaP = sbt("iotaP", [128, 1])
        pb64 = sbt("pb64", [128, 1])
        pb64b = sbt("pb64b", [128, 1])
        iotaS = sc1[:, 0:256]
        Itile = sc1[:, 256:512]
        toks = sbt("toks", [128, E, 2])
        toku = sbt("toku", [128, E, 2], U32)
        flatu = sbt("flatu", [128, E, 2], U32)
        rlrow = sbt("rlrow", [128, 4, 16])
        wtv = sbt("wtv", [128, 4])
        mge = sbt("mge", [16, 1], U32)
        mlt = sbt("mlt", [16, 1], U32)

        rls = srb[:, :, :].rearrange("p a b -> p (a b)").bitcast(F32)[:, 0:1024]
        selb = oTb[:, :, :].rearrange("p a b -> p (a b)").bitcast(F32)[:, 0:1024]
        gathf = xm[:, 0, :, :].rearrange("p a b -> p (a b)")
        eiT = xm[:, 1, :, :]
        h1g = xT[:, 0, :, :].rearrange("p a b -> p (a b)")
        h2b = xT[:, 1, :, :].rearrange("p a b -> p (a b)")[:, 0:4096].rearrange(
            "p (a b) -> p a b", a=16)
        eob = kvb[:, :, :].rearrange("p a b -> p (a b)")
        w3a = (wkT, wvT)
        w1a = wrT
        w2a = woT

        kps = [st.enter_context(nc.psum_tensor(f"kps{i}", [128, TC], F32)) for i in range(2)]
        vps = [st.enter_context(nc.psum_tensor(f"vps{i}", [128, TC], F32)) for i in range(2)]
        rps = st.enter_context(nc.psum_tensor("rps", [128, TC], F32))
        wps = [st.enter_context(nc.psum_tensor(f"wps{i}", [128, TC], F32)) for i in range(2)]
        lps = st.enter_context(nc.psum_tensor("lps", [128, 512], F32))

        sems = {}

        class Sem:
            __slots__ = ("h", "n")

            def __init__(self, name):
                self.h = st.enter_context(nc.semaphore(name))
                self.n = 0

        def S(name):
            if name not in sems:
                sems[name] = Sem("s" + name)
            return sems[name]

        tick = {e: S("tick_" + e) for e in ("vector", "scalar", "tensor", "gpsimd")}
        TV, TS, TT_, TG = tick["vector"], tick["scalar"], tick["tensor"], tick["gpsimd"]

        blk = st.enter_context(nc.Block())

        def on(eng, fn):
            streams[eng].append(fn)

        def wait(eng, sem, val=None):
            v = sem.n if val is None else val
            if v > 0:
                on(eng, lambda e, s=sem, v=v: e.wait_ge(s.h, v))

        def dstep(eng, sem, fn):
            def go(e, s=sem, fn=fn):
                fn(e).then_inc(s.h, 16)
            on(eng, go)
            sem.n += 16
            return sem.n

        def cstep(eng, fn, selfwait=True):
            t = tick[eng]
            if selfwait and eng != "tensor" and t.n > 0:
                on(eng, lambda e, s=t, v=t.n: e.wait_ge(s.h, v))

            def go(e, s=t, fn=fn):
                fn(e).then_inc(s.h, 1)
            on(eng, go)
            t.n += 1
            return t.n

        def bc(vec, n, ct=None):
            if ct is None:
                return vec[:, :].unsqueeze(2).to_broadcast([128, CT, n])
            return vec[:, ct:ct + 1].to_broadcast([128, n])

        for i, W in enumerate(Ws):
            wait("gpsimd", S("wcast"), 16 * i)
            dstep("gpsimd", S("wcast"), lambda e, W=W, i=i: e.dma_start(out=wtd[i], in_=W))
        for j in range(nchunks):
            wait("gpsimd", S("xbf"), 16 * j)
            dstep("gpsimd", S("xbf"), lambda e, j=j: e.dma_start(
                out=xbfd[j * TC:(j + 1) * TC, :], in_=xq[j * TC:(j + 1) * TC, :]))
        dstep("gpsimd", S("gset"), lambda e: e.dma_start(
            out=rw1T[:, :, :], in_=rw1.rearrange("(a p) h -> p a h", p=128)))
        dstep("gpsimd", S("gset"), lambda e: e.dma_start(out=rw2b[:, :], in_=rw2))
        cstep("gpsimd", lambda e: e.iota(iotaP[:, :], [[1, 1]], base=0,
                                         channel_multiplier=1,
                                         allow_small_or_imprecise_dtypes=True))
        cstep("gpsimd", lambda e: e.iota(pb64[:, :], [[1, 1]], base=0,
                                         channel_multiplier=64,
                                         allow_small_or_imprecise_dtypes=True))
        cstep("gpsimd", lambda e: e.iota(pb64b[:, :], [[1, 1]], base=64,
                                         channel_multiplier=64,
                                         allow_small_or_imprecise_dtypes=True))

        wait("sync", S("wcast"))
        for i, wT in enumerate((wkT, wvT, wrT, woT)):
            for ct in range(CT):
                dstep("sync", S("set"), lambda e, i=i, ct=ct, wT=wT: e.dma_start(
                    out=wT[:, ct, :], in_=wtd[i, :, ct * 128:(ct + 1) * 128], transpose=True))
        for src, dst in ((tmk, vtmk), (tmv, vtmv), (tmr, vtmr), (td, vdec), (tfi, vtf)):
            dstep("sync", S("set"), lambda e, src=src, dst=dst: e.dma_start(
                out=dst[:, :], in_=src.rearrange("(a p) -> p a", p=128),
                allow_slow_non_contiguous=True))
        for name, dst in (("ltri", c_ltri), ("shift1", c_shift), ("btri8", c_btri),
                          ("sel16to128", c_s16), ("sel128to16", c_s128),
                          ("identf", c_idf)):
            dstep("sync", S("set"), lambda e, name=name, dst=dst: e.dma_start(
                out=dst[:, :], in_=cst[name]))
        set_n = S("set").n

        zn = cstep("vector", lambda e: e.memset(sc1[:, :], 0.0), selfwait=False)
        wait("sync", TV, zn)
        dstep("sync", S("set"), lambda e: e.dma_start(
            out=idsd.rearrange("e t -> (e t)").rearrange(
                "(p c) -> p c", c=1024), in_=sc1[:, :]))
        for zi in range(4):
            dstep("sync", S("set"), lambda e, zi=zi: e.dma_start(
                out=eisc[zi // 2, (zi % 2) * 128:(zi % 2) * 128 + 128, :],
                in_=sc1[:, :].bitcast(BF16)[:, 0:1024]))

        wait("scalar", S("set"), S("set").n)
        cstep("scalar", lambda e: e.activation(vdec[:, :], vdec[:, :], AF.Exp))
        cstep("scalar", lambda e: e.activation(vdec[:, :], vdec[:, :], AF.Exp, scale=-1.0))
        ts_setup = cstep("scalar", lambda e: e.activation(vtf[:, :], vtf[:, :], AF.Exp))
        wait("vector", S("set"), S("set").n)
        cstep("vector", lambda e: e.memset(ones1[:, :], 1.0))
        cstep("vector", lambda e: e.memset(c_epswk[:, :], EPS_WK))
        cstep("vector", lambda e: e.memset(c_one1[:, :], 1.0))
        cstep("vector", lambda e: e.memset(ckv[:, :], 0.0))
        cstep("vector", lambda e: e.memset(ckc[:, :], 0.0))
        cstep("vector", lambda e: e.memset(xT[:, 1, :, XOF + TC - 1:XOF + TC], 0.0))
        cstep("vector", lambda e: e.memset(Mrun[:, :], -1e30))
        tv_setup = cstep("vector", lambda e: e.memset(mrun[:, :], 1e30))

        wait("tensor", S("set"), S("set").n)
        wait("tensor", S("gset"), S("gset").n)

        kdone, vdone, rdone = [], [], []
        kcS_of, srS_of = {}, {}
        kv_of = {}
        wfree = {}
        s10mm, s10_of, x1v_of = {}, {}, {}
        odone, hid_of, slu_of, rlm_of = [], [], [], []
        rtp_of, rlv_of, rtv_of = [], [], []
        gsw_of, acw_of, dbgx_of = {}, {}, {}
        rlw_of, rtw_of, dbgr_of = [], [], []
        xrw_of = {}
        oV_of = []
        premix_bnd, premix_done, premixG_done, rec_of = {}, {}, {}, {}
        oG_of, mult_of = {}, {}

        def xpose(j):
            rj = j % 2
            t0 = j * TC
            xts = S("xt0") if rj == 0 else S("xt1")
            wait("sync", S("xbf"), 16 * (j + 1))
            if j >= 1:
                wait("sync", TV, premix_bnd[j - 1])
            for ct in range(CT):
                dstep("sync", xts, lambda e, ct=ct, rj=rj, t0=t0: e.dma_start(
                    out=xT[:, rj, ct, XOF:XOF + TC],
                    in_=xbfd[t0:t0 + TC, ct * 128:(ct + 1) * 128], transpose=True))

        def premix(j):
            rj = j % 2
            xts = S("xt0") if rj == 0 else S("xt1")
            wait("vector", xts)
            if j == 0:
                wait("vector", TV, tv_setup)
            bnd_n = cstep("vector", lambda e, rj=rj: e.tensor_copy(
                xT[:, rj, :, XOF - 1:XOF], xT[:, 1 - rj, :, XOF + TC - 1:XOF + TC]))
            premix_bnd[j] = bnd_n
            wait("gpsimd", xts)
            wait("gpsimd", TV, bnd_n)
            dif_n = cstep("gpsimd", lambda e, rj=rj: e.tensor_tensor(
                xm[:, 1, :, :], xT[:, rj, :, XOF:XOF + TC],
                xT[:, rj, :, XOF - 1:XOF + TC - 1], OP.subtract))
            wait("vector", TG, dif_n)
            if j > 0:
                wait("vector", TT_, rdone[j - 1])
            for ct in range(CT):
                cstep("vector", lambda e, rj=rj, ct=ct: e.scalar_tensor_tensor(
                    xm[:, 0, ct, :], xm[:, 1, ct, :], vtmk[:, ct:ct + 1],
                    xT[:, rj, ct, XOF - 1:XOF + TC - 1], OP.mult, OP.add))
            if j > 0:
                wait("vector", TT_, vdone[j - 1])
            for ct in range(CT):
                cstep("vector", lambda e, rj=rj, ct=ct: e.scalar_tensor_tensor(
                    xmB[:, ct, :], xm[:, 1, ct, :], vtmv[:, ct:ct + 1],
                    xT[:, rj, ct, XOF - 1:XOF + TC - 1], OP.mult, OP.add))
            premix_done[j] = TV.n

        def fmain(j):
            rj = j % 2
            wait("tensor", TV, premix_done[j])
            for dt in range(CT):
                fk = (j, dt - 2) if dt >= 2 else (j - 1, dt + 6)
                if fk in kcS_of:
                    wait("tensor", TS, kcS_of[fk])
                for cc in range(CT):
                    kn = cstep("tensor", lambda e, dt=dt, cc=cc: e.matmul(
                        kps[dt % 2][:, :], wkT[:, cc, dt * 128:(dt + 1) * 128],
                        xm[:, 0, cc, :], start=(cc == 0), stop=(cc == CT - 1)))
                wait("scalar", TT_, kn)
                if j == 0 and dt == 0:
                    wait("scalar", TS, ts_setup)
                kcS_of[(j, dt)] = cstep("scalar", lambda e, dt=dt: e.activation(
                    kcb[:, dt, :], kps[dt % 2][:, :], AF.Exp))
            kdone.append(TT_.n)

            for dt in range(CT):
                fk = (j, dt - 2) if dt >= 2 else (j - 1, dt + 6)
                if fk in kv_of:
                    wait("tensor", TV, kv_of[fk])
                for cc in range(CT):
                    vn = cstep("tensor", lambda e, dt=dt, cc=cc: e.matmul(
                        vps[dt % 2][:, :], wvT[:, cc, dt * 128:(dt + 1) * 128],
                        xmB[:, cc, :], start=(cc == 0), stop=(cc == CT - 1)))
                wait("vector", TS, kcS_of[(j, dt)])
                cstep("vector", lambda e, dt=dt: e.tensor_scalar_min(
                    kcb[:, dt, :], kcb[:, dt, :], E60))
                wait("vector", TT_, vn)
                kv_of[(j, dt)] = cstep("vector", lambda e, dt=dt: e.tensor_tensor(
                    kvb[:, dt, :], kcb[:, dt, :], vps[dt % 2][:, :], OP.mult))
            vdone.append(TT_.n)

            wait("vector", TT_, kdone[j])
            for ct in range(CT):
                mixR_n = cstep("vector", lambda e, rj=rj, ct=ct: e.scalar_tensor_tensor(
                    xm[:, 0, ct, :], xm[:, 1, ct, :], vtmr[:, ct:ct + 1],
                    xT[:, rj, ct, XOF - 1:XOF + TC - 1], OP.mult, OP.add))

            wait("tensor", TV, mixR_n)
            for dt in range(CT):
                fk = (j, dt - 1) if dt >= 1 else (j - 1, 7)
                if fk in srS_of:
                    wait("tensor", TS, srS_of[fk])
                for cc in range(CT):
                    rn = cstep("tensor", lambda e, dt=dt, cc=cc: e.matmul(
                        rps[:, :], wrT[:, cc, dt * 128:(dt + 1) * 128],
                        xm[:, 0, cc, :], start=(cc == 0), stop=(cc == CT - 1)))
                wait("scalar", TT_, rn)
                srS_of[(j, dt)] = cstep("scalar", lambda e, dt=dt: e.activation(
                    srb[:, dt, :], rps[:, :], AF.Sigmoid))
            rdone.append(TT_.n)

            for si, (src, car, res) in enumerate(((kvb, ckv, kvb), (kcb, ckc, kcb))):
                cstep("vector", lambda e, car=car: e.tensor_copy(
                    ybuf[:, :, 0:1], car[:, :].unsqueeze(2)))
                for dt in range(CT):
                    if si == 0:
                        wait("vector", TV, kv_of[(j, dt)])
                    cstep("vector", lambda e, dt=dt, src=src:
                          e.tensor_tensor_scan(
                              ybuf[:, dt, 1:SCN + 1], bc(vdec, SCN, dt),
                              src[:, dt, 0:SCN], ybuf[:, dt, 0:1],
                              OP.mult, OP.add))
                cstep("vector", lambda e, car=car: e.tensor_copy(
                    car[:, :].unsqueeze(2), ybuf[:, :, SCN:SCN + 1]))
                for dt in range(CT):
                    wk_n = cstep("vector", lambda e, dt=dt, res=res:
                                 e.scalar_tensor_tensor(
                        res[:, dt, :], res[:, dt, :], vtf[:, dt:dt + 1],
                        ybuf[:, dt, 0:SCN], OP.mult, OP.add))
            wait("scalar", TV, wk_n)
            cstep("scalar", lambda e: e.activation(
                ybuf[:, :, 0:SCN], kcb[:, :, :], AF.Ln, bias=c_epswk[:, :]))
            rec_of[j] = cstep("scalar", lambda e: e.activation(
                ybuf[:, :, 0:SCN], ybuf[:, :, 0:SCN], AF.Exp, scale=-1.0))

        def ftail(j):
            wait("vector", TS, rec_of[j])
            mult_of[j] = cstep("vector", lambda e: e.tensor_tensor(
                kvb[:, :, :], kvb[:, :, :], ybuf[:, :, 0:SCN], OP.mult))
            wait("vector", TS, srS_of[(j, CT - 1)])
            if j > 0:
                wait("vector", TT_, odone[j - 1])
            cstep("vector", lambda e: e.tensor_tensor(
                oTb[:, :, :], kvb[:, :, :], srb[:, :, :], OP.mult))
            oV_of.append(TV.n)

        def back(j):
            t0 = j * TC
            wait("tensor", TV, oV_of[j])
            if j > 0:
                wait("tensor", TS, slu_of[j - 1])
            for h in range(2):
                ta = t0 + h * 256
                prev = (j, 1) if h == 1 else (j - 1, 3)
                if prev in s10_of:
                    wait("scalar", TV, s10_of[prev])
                xrw_of[(j, h)] = dstep("scalar", S("xrw"), lambda e, ta=ta: e.dma_start(
                    out=xrow[:, :, :], in_=xq[ta:ta + 256, :].rearrange(
                        "(a p) c -> p a c", p=128)))
                for hi in range(2):
                    sub = 2 * h + hi
                    for dc in range(2):
                        fk = (j, sub - 1, dc) if sub >= 1 else (j - 1, 3, dc)
                        if fk in wfree:
                            wait("tensor", TV, wfree[fk])
                        for cc in range(CT):
                            cstep("tensor", lambda e, sub=sub, dc=dc, cc=cc: e.matmul(
                                wps[dc][:, :], oTb[:, cc, sub * 128:(sub + 1) * 128],
                                woT[:, cc, dc * 512:(dc + 1) * 512],
                                start=(cc == 0), stop=(cc == CT - 1)))
                    s10mm[(j, sub)] = TT_.n
                    wait("vector", S("xrw"), xrw_of[(j, h)])
                    wait("vector", TT_, s10mm[(j, sub)])
                    prevs = (j, sub - 2) if sub >= 2 else (j - 1, sub + 2)
                    if prevs in gsw_of:
                        wait("vector", S("gsw%d" % (sub % 2)), gsw_of[prevs])
                        wait("vector", S("acw%d" % (sub % 2)), acw_of[prevs])
                        if "x1" in dbg:
                            wait("vector", S("dbgx%d" % (sub % 2)), dbgx_of[prevs])
                    for dc in range(2):
                        wfree[(j, sub, dc)] = cstep(
                            "vector", lambda e, hi=hi, dc=dc: e.tensor_tensor(
                                x1p[:, hi, dc * 512:(dc + 1) * 512], wps[dc][:, :],
                                xrow[:, hi, dc * 512:(dc + 1) * 512], OP.add))
                    s10_of[(j, sub)] = TV.n
                    wait("scalar", TV, s10_of[(j, sub)])
                    cstep("scalar", lambda e, hi=hi: e.activation(
                        sc1[:, :], x1p[:, hi, :], AF.Square, accum_out=lsq[:, hi:hi + 1]))
                    lns_n = cstep("scalar", lambda e, hi=hi: e.activation(
                        sc1[:, :], x1p[:, hi, :], AF.Identity, scale=1.0 / C,
                        accum_out=lmn[:, hi:hi + 1]))
                    wait("vector", TS, lns_n)
                    cstep("vector", lambda e, hi=hi: e.tensor_scalar(
                        lrs[:, hi:hi + 1], lsq[:, hi:hi + 1], 1.0 / C, None, OP.mult))
                    cstep("vector", lambda e, hi=hi: e.tensor_tensor(
                        r1[:, 0:1], lmn[:, hi:hi + 1], lmn[:, hi:hi + 1], OP.mult))
                    cstep("vector", lambda e, hi=hi: e.tensor_tensor(
                        lrs[:, hi:hi + 1], lrs[:, hi:hi + 1], r1[:, 0:1], OP.subtract))
                    var_n = cstep("vector", lambda e, hi=hi: e.tensor_scalar(
                        lrs[:, hi:hi + 1], lrs[:, hi:hi + 1], EPS_LN, None, OP.add))
                    wait("scalar", TV, var_n)
                    sqt_n = cstep("scalar", lambda e, hi=hi: e.activation(
                        lrs[:, hi:hi + 1], lrs[:, hi:hi + 1], AF.Sqrt))
                    wait("vector", TS, sqt_n)
                    cstep("vector", lambda e, hi=hi: e.reciprocal(
                        lrs[:, hi:hi + 1], lrs[:, hi:hi + 1]))
                    cstep("vector", lambda e, hi=hi: e.tensor_scalar(
                        r2[:, 0:1], lmn[:, hi:hi + 1], -1.0, None, OP.mult))
                    x1v_of[(j, sub)] = cstep("vector", lambda e, hi=hi: e.tensor_scalar(
                        x1p[:, hi, :], x1p[:, hi, :], r2[:, 0:1], lrs[:, hi:hi + 1],
                        OP.add, OP.mult))
                    wait("sync", TV, x1v_of[(j, sub)])
                    acw_of[(j, sub)] = dstep("sync", S("acw%d" % (sub % 2)),
                                             lambda e, ta=ta, hi=hi: e.dma_start(
                        out=accum[ta + hi * 128:ta + (hi + 1) * 128, :],
                        in_=x1p[:, hi, :]))
                    wait("gpsimd", TV, x1v_of[(j, sub)])
                    gsw_of[(j, sub)] = dstep("gpsimd", S("gsw%d" % (sub % 2)),
                                             lambda e, ta=ta, hi=hi: e.dma_start(
                        out=gsrc[ta + hi * 128:ta + (hi + 1) * 128, :],
                        in_=x1p[:, hi, :]))
                    if "x1" in dbg:
                        dbgx_of[(j, sub)] = dstep("sync", S("dbgx%d" % (sub % 2)),
                                                  lambda e, ta=ta, hi=hi: e.dma_start(
                            out=dbg["x1"][ta + hi * 128:ta + (hi + 1) * 128, :],
                            in_=x1p[:, hi, :]))

            odone.append(s10mm[(j, 3)])

            wait("sync", S("gsw0"), gsw_of[(j, 2)])
            wait("sync", S("gsw1"), gsw_of[(j, 3)])
            if j > 0:
                wait("sync", TT_, hid_of[j - 1])
            for ct in range(CT):
                dstep("sync", S("x1t"), lambda e, ct=ct, t0=t0: e.dma_start(
                    out=x1T[:, ct, :], in_=gsrc[t0:t0 + TC, ct * 128:(ct + 1) * 128],
                    transpose=True))
            wait("tensor", S("x1t"))
            if j == 0:
                wait("tensor", S("gset"), 32)
            wait("tensor", TV, wfree[(j, 3, 0)])
            for cc in range(CT):
                hn = cstep("tensor", lambda e, cc=cc: e.matmul(
                    wps[0][:, :], rw1T[:, cc, :], x1T[:, cc, :],
                    start=(cc == 0), stop=(cc == CT - 1)))
            hid_of.append(hn)
            wait("scalar", TT_, hn)
            if j > 0:
                wait("scalar", TT_, rlm_of[j - 1])
            slu_of.append(cstep("scalar", lambda e: e.activation(
                hidT[:, :], wps[0][:, :], AF.Silu)))

        def backB(j):
            t0 = j * TC
            wait("tensor", TS, slu_of[j])
            if j > 0:
                wait("tensor", TV, rtv_of[j - 1])
            rlm_of.append(cstep("tensor", lambda e: e.matmul(
                lps[0:16, :], rw2b[:, :], hidT[:, :], start=True, stop=True)))
            wait("vector", TT_, rlm_of[j])
            if j > 0:
                wait("vector", S("rlw"), rlw_of[j - 1])
                wait("vector", TT_, rtp_of[j - 1])
                if "rl" in dbg:
                    wait("vector", S("dbgr"), dbgr_of[j - 1])
            rlv_of.append(cstep("vector", lambda e: e.tensor_copy(
                rlc[:, :], lps[0:16, :])))
            cstep("vector", lambda e: e.reduce_max(t16a[:, :], rlc[:, :], axis=AX.X))
            cstep("vector", lambda e: e.tensor_max(Mrun[:, :], Mrun[:, :], t16a[:, :]))
            cstep("vector", lambda e: e.tensor_reduce(t16b[:, :], rlc[:, :], AX.X, OP.min))
            mm_n = cstep("vector", lambda e: e.tensor_tensor(
                mrun[:, :], mrun[:, :], t16b[:, :], OP.min))
            wait("sync", TV, mm_n)
            rlw_of.append(dstep("sync", S("rlw"), lambda e, t0=t0: e.dma_start(
                out=rld[:, t0:t0 + TC], in_=rlc[:, :])))
            if "rl" in dbg:
                dbgr_of.append(dstep("sync", S("dbgr"), lambda e, t0=t0: e.dma_start(
                    out=dbg["rl"][:, t0:t0 + TC], in_=rlc[:, :])))
            wait("tensor", TV, rlv_of[j])
            for a_ in range(4):
                rt_n = cstep("tensor", lambda e, a_=a_: e.transpose(
                    lps[:, a_ * 16:(a_ + 1) * 16],
                    rlc[:, a_ * 128:(a_ + 1) * 128], c_idf[0:16, 0:16]))
            rtp_of.append(rt_n)
            wait("vector", TT_, rt_n)
            if j > 0:
                wait("vector", S("rtw"), rtw_of[j - 1])
            rtv_of.append(cstep("vector", lambda e: e.tensor_copy(
                rlt[:, :, :], lps[:, 0:64].rearrange("p (a b) -> p a b", a=4))))
            wait("sync", TV, rtv_of[j])
            rtw_of.append(dstep("sync", S("rtw"), lambda e, t0=t0: e.dma_start(
                out=rlg[t0:t0 + TC, :].rearrange("(a p) e -> p a e", p=128),
                in_=rlt[:, :, :])))

        xpose(0)
        premix(0)
        if nchunks > 1:
            xpose(1)
        for j in range(nchunks):
            fmain(j)
            if j + 1 < nchunks:
                premix(j + 1)
            if j > 1:
                backB(j - 2)
            if j > 0:
                back(j - 1)
            ftail(j)
            if j + 2 < nchunks:
                xpose(j + 2)
        backB(nchunks - 2)
        back(nchunks - 1)
        backB(nchunks - 1)

        if upto == "x1" or not full:
            for sname in ("dbgx0", "dbgx1", "acw0", "acw1", "gsw0", "gsw1",
                          "rtw", "rlw", "dbgr"):
                wait("sync", S(sname))
            _emit(nc, blk, streams)
            return nc, dbg

        ph1_tt = TT_.n
        w1l_of, w2l_of, w3l_of, w3l1_of = {}, {}, {}, {}
        g_done = {}

        def wload(e_, fb):
            par = fb % 2
            pk = (e_, fb - 2) if fb >= 2 else (e_ - 1, fb + 2)
            if pk in g_done:
                wait("gpsimd", TT_, g_done[pk])
            w1l_of[(e_, fb)] = dstep("gpsimd", S("w1l0" if par == 0 else "w1l1"),
                                     lambda e, e_=e_, fb=fb, par=par: e.dma_start(
                out=w1a[:, :, par * 512:(par + 1) * 512],
                in_=w1d[e_, :, fb * 512:(fb + 1) * 512].rearrange(
                    "(a p) f -> p a f", p=128)))
            w2l_of[(e_, fb)] = dstep("gpsimd", S("w2l0" if par == 0 else "w2l1"),
                                     lambda e, e_=e_, fb=fb, par=par: e.dma_start(
                out=w2a[:, :, par * 512:(par + 1) * 512],
                in_=w2d[e_, :, fb * 512:(fb + 1) * 512].rearrange(
                    "(a p) f -> p a f", p=128)))

        wait("gpsimd", TT_, ph1_tt)
        w3l1_of[0] = dstep("gpsimd", S("w3l"), lambda e: e.dma_start(
            out=wkT[:, :, :],
            in_=w3d[0, 0:1024, :].rearrange("(a p) d -> p a d", p=128)))
        w3l_of[0] = dstep("gpsimd", S("w3l"), lambda e: e.dma_start(
            out=wvT[:, :, :],
            in_=w3d[0, 1024:2048, :].rearrange("(a p) d -> p a d", p=128)))
        wload(0, 0)
        wload(0, 1)

        wait("sync", S("rlw"), rlw_of[-1])
        wait("sync", TV, TV.n)
        rlsl_n = dstep("sync", S("rlsl"), lambda e: e.dma_start(
            out=rls, in_=rld.rearrange("e (s t) -> (e s) t", s=8)))
        wait("vector", S("rlsl"), rlsl_n)

        neg_n = cstep("vector", lambda e: e.tensor_scalar(
            t16a[:, :], Mrun[:, :], -1.0, None, OP.mult))
        wait("tensor", TV, max(neg_n, rlv_of[-1], rtv_of[-1]))
        wait("tensor", TS, kcS_of[(nchunks - 1, CT - 1)])
        bN = cstep("tensor", lambda e: e.matmul(
            kps[0][:, 0:1], c_s16[:, :], t16a[:, :], start=True, stop=True))
        wait("vector", TT_, bN)
        r1_n = cstep("vector", lambda e: e.tensor_copy(r1[:, 0:1], kps[0][:, 0:1]))

        wait("scalar", TV, r1_n)
        z1 = cstep("scalar", lambda e: e.activation(
            sc1[:, :], rls, AF.Exp, bias=r1[:, 0:1], accum_out=cntp[:, 0:1]))
        wait("tensor", TS, z1)
        z2 = cstep("tensor", lambda e: e.matmul(
            kps[0][0:16, 0:1], c_s128[:, :], cntp[:, 0:1], start=True, stop=True))
        wait("vector", TT_, z2)
        cstep("vector", lambda e: e.tensor_copy(z16[:, :], kps[0][0:16, 0:1]))

        wait("vector", TS, z1)
        cstep("vector", lambda e: e.tensor_scalar(
            lo16[:, :], mrun[:, :], -1.0, None, OP.add))
        cstep("vector", lambda e: e.tensor_copy(hi16[:, :], Mrun[:, :]))
        for it in range(BIS_ITERS):
            m_n = cstep("vector", lambda e: e.tensor_scalar(
                mid16[:, :], lo16[:, :], hi16[:, 0:1], 0.5, OP.add, OP.mult))
            wait("tensor", TV, m_n)
            b_n = cstep("tensor", lambda e: e.matmul(
                kps[0][:, 0:1], c_s16[:, :], mid16[:, :], start=True, stop=True))
            wait("vector", TT_, b_n)
            cstep("vector", lambda e: e.tensor_copy(r2[:, 0:1], kps[0][:, 0:1]))
            c_n = cstep("vector", lambda e: e.tensor_scalar(
                sc1[:, :], rls, r2[:, 0:1], 0.0, OP.is_gt, OP.add,
                accum_out=cntp[:, 0:1]))
            wait("tensor", TV, c_n)
            cm = cstep("tensor", lambda e: e.matmul(
                kps[0][0:16, 0:1], c_s128[:, :], cntp[:, 0:1], start=True, stop=True))
            wait("vector", TT_, cm)
            cstep("vector", lambda e: e.tensor_scalar(
                mge[:, :], kps[0][0:16, 0:1], float(CAP), None, OP.is_ge))
            cstep("vector", lambda e: e.select(
                lo16[:, :], mge[:, :], mid16[:, :], lo16[:, :]))
            cstep("vector", lambda e: e.select(
                hi16[:, :], mge[:, :], hi16[:, :], mid16[:, :]))
        th_n = cstep("vector", lambda e: e.tensor_copy(mid16[:, :], hi16[:, :]))
        wait("tensor", TV, th_n)
        b_n = cstep("tensor", lambda e: e.matmul(
            kps[0][:, 0:1], c_s16[:, :], mid16[:, :], start=True, stop=True))
        wait("vector", TT_, b_n)
        cstep("vector", lambda e: e.tensor_copy(r2[:, 0:1], kps[0][:, 0:1]))
        c_n = cstep("vector", lambda e: e.tensor_scalar(
            sc1[:, :], rls, r2[:, 0:1], 0.0, OP.is_gt, OP.add, accum_out=cntp[:, 0:1]))
        wait("tensor", TV, c_n)
        cm = cstep("tensor", lambda e: e.matmul(
            kps[0][0:16, 0:1], c_s128[:, :], cntp[:, 0:1], start=True, stop=True))
        wait("vector", TT_, cm)
        cstep("vector", lambda e: e.tensor_copy(n116[:, :], kps[0][0:16, 0:1]))
        q_n = cstep("vector", lambda e: e.tensor_scalar(
            q16[:, :], n116[:, :], -1.0, float(CAP), OP.mult, OP.add))

        wait("vector", TT_, odone[-1])
        cstep("vector", lambda e: e.tensor_scalar(
            selb, rls, r2[:, 0:1], None, OP.is_gt))
        tm_n = cstep("vector", lambda e: e.tensor_scalar(
            sc1[:, :], rls, r2[:, 0:1], 0.0, OP.is_equal, OP.add,
            accum_out=cntp[:, 0:1]))
        cstep("vector", lambda e: e.tensor_tensor_scan(
            sc1[:, :], c_one1[:, 0:1].to_broadcast([128, 1024]), sc1[:, :],
            0.0, OP.mult, OP.add))
        wait("tensor", TV, tm_n)
        ob_n = cstep("tensor", lambda e: e.matmul(
            kps[1][:, 0:1], c_btri[:, :], cntp[:, 0:1], start=True, stop=True))
        wait("vector", TT_, ob_n)
        cstep("vector", lambda e: e.tensor_copy(r3[:, 0:1], kps[1][:, 0:1]))
        cstep("vector", lambda e: e.tensor_scalar(
            sc1[:, :], sc1[:, :], r3[:, 0:1], None, OP.add))
        qb_n = cstep("vector", lambda e: e.tensor_copy(t16a[:, :], q16[:, :]))
        wait("tensor", TV, qb_n)
        qb2 = cstep("tensor", lambda e: e.matmul(
            kps[1][:, 0:1], c_s16[:, :], t16a[:, :], start=True, stop=True))
        wait("vector", TT_, qb2)
        cstep("vector", lambda e: e.tensor_copy(r4[:, 0:1], kps[1][:, 0:1]))
        cstep("vector", lambda e: e.tensor_scalar(
            sc1[:, :], sc1[:, :], r4[:, 0:1], None, OP.is_le))
        cstep("vector", lambda e: e.scalar_tensor_tensor(
            sc1[:, :], rls, r2[:, 0:1], sc1[:, :], OP.is_equal, OP.mult))
        sel_n = cstep("vector", lambda e: e.tensor_tensor(selb, selb, sc1[:, :], OP.add))

        wait("scalar", TV, sel_n)
        es_n = cstep("scalar", lambda e: e.activation(
            sc1[:, :], rls, AF.Exp, bias=r1[:, 0:1]))
        wait("vector", TS, es_n)
        cstep("vector", lambda e: e.tensor_tensor(sc1[:, :], sc1[:, :], selb, OP.mult))
        d_n = cstep("vector", lambda e: e.tensor_scalar(
            sc1[:, :], sc1[:, :], 1.0, 0.0, OP.mult, OP.add, accum_out=cntp[:, 0:1]))
        wait("tensor", TV, d_n)
        dm = cstep("tensor", lambda e: e.matmul(
            kps[0][0:16, 0:1], c_s128[:, :], cntp[:, 0:1], start=True, stop=True))
        wait("vector", TT_, dm)
        cstep("vector", lambda e: e.tensor_copy(dn16[:, :], kps[0][0:16, 0:1]))
        cstep("vector", lambda e: e.tensor_scalar(
            t16b[:, :], z16[:, :], EPS_W, None, OP.mult))
        cstep("vector", lambda e: e.tensor_tensor(dn16[:, :], dn16[:, :], t16b[:, :], OP.add))
        rcd_n = cstep("vector", lambda e: e.reciprocal(dn16[:, :], dn16[:, :]))

        nm_n = cstep("vector", lambda e: e.tensor_scalar(
            t16a[:, :], Mrun[:, :], -1.0, None, OP.mult))
        for src, src_n, dst in ((t16a, nm_n, nM128), (dn16, rcd_n, rd128)):
            wait("tensor", TV, TV.n)
            t_n = cstep("tensor", lambda e, src=src: e.transpose(
                kps[1][0:1, 0:16], src[:, :], c_idf[0:16, 0:16]))
            wait("vector", TT_, t_n)
            rw_n = cstep("vector", lambda e: e.tensor_copy(row16[:, :], kps[1][0:1, 0:16]))
            wait("tensor", TV, rw_n)
            b2 = cstep("tensor", lambda e: e.matmul(
                kps[1][:, 0:16], ones1[:, :], row16[:, :], start=True, stop=True))
            wait("vector", TT_, b2)
            cstep("vector", lambda e, dst=dst: e.tensor_copy(dst[:, :], kps[1][:, 0:16]))

        wait("gpsimd", TV, TV.n)
        wait("gpsimd", TS, TS.n)
        ki_n = cstep("gpsimd", lambda e: e.iota(
            sc1[:, :], [[0, 16], [1, 64]], base=0, channel_multiplier=0,
            allow_small_or_imprecise_dtypes=True))
        wait("vector", TG, ki_n)
        cstep("vector", lambda e: e.tensor_scalar(
            sc1[:, :], sc1[:, :], -1.0, 64.0, OP.mult, OP.add))
        key_n = cstep("vector", lambda e: e.tensor_tensor(sc1[:, :], sc1[:, :], selb, OP.mult))
        wait("sync", TV, key_n)
        keyw_n = dstep("sync", S("keyw"), lambda e: e.dma_start(
            out=keyd.rearrange("e (s t) -> (e s) t", s=8), in_=sc1[:, :]))

        wait("gpsimd", S("set"), S("set").n)
        wait("gpsimd", S("keyw"), keyw_n)
        cstep("gpsimd", lambda e: e.iota(iotaS[:, :], [[1, 256]], base=0,
                                         channel_multiplier=0,
                                         allow_small_or_imprecise_dtypes=True))
        wait("vector", TG, TG.n)
        ids_of = {}
        tokv_of = {}
        idsd_flat = idsd.rearrange("e t -> (e t)").unsqueeze(1)
        for e_ in range(E):
            wait("sync", S("keyw"), keyw_n)
            if e_ > 0:
                wait("sync", TV, ids_of[e_ - 1])
            kl = dstep("sync", S("keyl"), lambda e, e_=e_: e.dma_start(
                out=keyt[:, :], in_=keyd[e_].rearrange("(p c) -> p c", c=64)))
            wait("vector", S("keyl"), kl)
            wait("vector", S("idsw"), S("idsw").n)
            np_n = cstep("vector", lambda e: e.tensor_scalar(
                keytab[:, :], keyt[:, :], 0.0, 0.0, OP.is_gt, OP.add,
                accum_out=cntp[:, 0:1]))
            for it in range(8):
                cstep("vector", lambda e, it=it: e.max(
                    out=keytab[:, it * 8:(it + 1) * 8], in_=keyt[:, :]))
                cstep("vector", lambda e, it=it: e.match_replace(
                    out=keyt[:, :], in_to_replace=keytab[:, it * 8:(it + 1) * 8],
                    in_values=keyt[:, :], imm_value=-1.0))
            ids_n = cstep("vector", lambda e: e.tensor_scalar(
                idst[:, :], keytab[:, :], pb64b[:, 0:1], -1.0, OP.subtract, OP.mult))
            ids_of[e_] = ids_n
            wait("tensor", TV, np_n)
            o_n = cstep("tensor", lambda e: e.matmul(
                kps[0][:, 0:1], c_ltri[:, :], cntp[:, 0:1], start=True, stop=True))
            wait("vector", TT_, o_n)
            cstep("vector", lambda e: e.tensor_copy(r3[:, 0:1], kps[0][:, 0:1]))
            a_n = cstep("vector", lambda e: e.tensor_scalar(
                r4[:, 0:1], r3[:, 0:1], pb64[:, 0:1], -1.0, OP.subtract, OP.mult))
            wait("tensor", TV, a_n)
            s_n = cstep("tensor", lambda e: e.matmul(
                kps[1][:, 0:1], c_shift[:, :], r4[:, 0:1], start=True, stop=True))
            wait("vector", TT_, s_n)
            cstep("vector", lambda e: e.tensor_copy(r2[:, 0:1], kps[1][:, 0:1]))
            d2_n = cstep("vector", lambda e: e.tensor_tensor(
                r4[:, 0:1], r4[:, 0:1], r2[:, 0:1], OP.subtract))
            i_n = cstep("vector", lambda e: e.tensor_scalar(
                Itile[:, :], iotaS[:, :], r3[:, 0:1], None, OP.is_ge))
            for sc_ in range(2):
                wait("tensor", TV, TV.n)
                f_n = cstep("tensor", lambda e, sc_=sc_: e.matmul(
                    kps[0][:, 0:1], Itile[:, sc_ * 128:(sc_ + 1) * 128], r4[:, 0:1],
                    start=True, stop=True))
                wait("vector", TT_, f_n)
                cstep("vector", lambda e: e.tensor_copy(r2[:, 0:1], kps[0][:, 0:1]))
                cstep("vector", lambda e: e.tensor_tensor(
                    r2[:, 0:1], r2[:, 0:1], iotaP[:, 0:1], OP.add))
                cstep("vector", lambda e, e_=e_, sc_=sc_: e.tensor_scalar(
                    r2[:, 0:1], r2[:, 0:1], float(e_ * T + sc_ * 128), None, OP.add))
                cv_n = cstep("vector", lambda e, e_=e_, sc_=sc_: e.tensor_copy(
                    flatu[:, e_, sc_:sc_ + 1], r2[:, 0:1]))
                if sc_ == 0:
                    wait("gpsimd", TV, ids_n)
                    dstep("gpsimd", S("idsw"), lambda e, e_=e_: e.dma_start(
                        out=idsd[e_].rearrange("(p r) -> p r", r=64), in_=idst[:, :]))
                wait("gpsimd", TV, cv_n)
                wait("gpsimd", S("idsw"), S("idsw").n)
                dstep("gpsimd", S("res"), lambda e, e_=e_, sc_=sc_: e.indirect_dma_start(
                    toks[:, e_, sc_:sc_ + 1], None, idsd_flat,
                    IndirectOffsetOnAxis(flatu[:, e_, sc_:sc_ + 1], 0)))
            wait("vector", S("res"), S("res").n)
            tokv_of[e_] = cstep("vector", lambda e, e_=e_: e.tensor_copy(
                toku[:, e_, :], toks[:, e_, :]))
        tok_n = tokv_of[E - 1]

        if upto == "topk":
            wait("sync", TV, tok_n)
            dstep("sync", S("dtk"), lambda e: e.dma_start(
                out=dbg["tok"], in_=toks[:, :, :].rearrange("p a b -> p (a b)")))
            for i, src in enumerate((mid16, n116, q16, z16, dn16, Mrun, mrun, lo16)):
                cstep("vector", lambda e, i=i, src=src: e.tensor_copy(
                    sc1[0:16, i:i + 1], src[:, :]))
            thd = TV.n
            wait("sync", TV, thd)
            dstep("sync", S("dtk"), lambda e: e.dma_start(
                out=dbg["th"], in_=sc1[0:16, 0:8]))
            for sname in ("dbgx0", "dbgx1", "acw0", "acw1", "gsw0", "gsw1",
                          "rtw", "rlw", "dbgr", "dtk"):
                wait("sync", S(sname))
            _emit(nc, blk, streams)
            return nc, dbg

        wait("gpsimd", S("gsw0"), S("gsw0").n)
        wait("gpsimd", S("gsw1"), S("gsw1").n)
        wait("gpsimd", S("rtw"), S("rtw").n)
        wait("gpsimd", TV, tokv_of[0])

        eit_of = {}
        eiw_of = {}
        ei_used = {}
        h2_used = {}
        silu3_of = {}
        h2v_of = {}
        numer_of = {}
        wt_of = {}
        eo7_of = {}
        rg_of, rlgg_of = {}, {}
        scat_prev = 0

        eiTs = (xm[:, 1, :, :], xm[:, 0, :, :])
        land = ybuf[:, :, :].rearrange("p a b -> p (a b)").bitcast(BF16)
        eop = None
        eov_last = {}

        for sc_ in range(2):
            rg_of[(0, sc_)] = dstep("gpsimd", S("rg0"),
                                    lambda e, sc_=sc_: e.indirect_dma_start(
                land[:, sc_ * 1024:(sc_ + 1) * 1024], None, gsrc,
                IndirectOffsetOnAxis(toku[:, 0, sc_:sc_ + 1], 0)))
            rlgg_of[(0, sc_)] = dstep("gpsimd", S("rlgg"),
                                      lambda e, sc_=sc_: e.indirect_dma_start(
                rlrow[:, sc_, :], None, rlg,
                IndirectOffsetOnAxis(toku[:, 0, sc_:sc_ + 1], 0)))

        for e_ in range(E):
            par3 = e_ % 2
            eiTc = eiTs[par3]
            if e_ + 1 < E:
                np1 = (e_ + 1) % 2
                wait("gpsimd", TV, tokv_of[e_ + 1])
                if e_ >= 1:
                    wait("gpsimd", S("eiw%d" % np1), eiw_of.get(e_ - 1, 0))
                    wait("gpsimd", TS, numer_of[e_ - 1])
                for sc_ in range(2):
                    rg_of[(e_ + 1, sc_)] = dstep(
                        "gpsimd", S("rg%d" % np1),
                        lambda e, e_=e_, sc_=sc_, np1=np1: e.indirect_dma_start(
                            land[:, np1 * 2048 + sc_ * 1024:
                                 np1 * 2048 + (sc_ + 1) * 1024], None, gsrc,
                            IndirectOffsetOnAxis(toku[:, e_ + 1, sc_:sc_ + 1], 0)))
                    rlgg_of[(e_ + 1, sc_)] = dstep(
                        "gpsimd", S("rlgg"),
                        lambda e, e_=e_, sc_=sc_, np1=np1: e.indirect_dma_start(
                            rlrow[:, np1 * 2 + sc_, :], None, rlg,
                            IndirectOffsetOnAxis(toku[:, e_ + 1, sc_:sc_ + 1], 0)))

            if e_ > 0:
                wload(e_, 0)
                wload(e_, 1)
                wait("gpsimd", TT_, eo7_of[e_ - 1])
                w3l1_of[e_] = dstep("gpsimd", S("w3l"), lambda e, e_=e_: e.dma_start(
                    out=wkT[:, :, :],
                    in_=w3d[e_, 0:1024, :].rearrange("(a p) d -> p a d", p=128)))
                wait("gpsimd", TT_, h2_used[e_ - 1])
                w3l_of[e_] = dstep("gpsimd", S("w3l"), lambda e, e_=e_: e.dma_start(
                    out=wvT[:, :, :],
                    in_=w3d[e_, 1024:2048, :].rearrange("(a p) d -> p a d", p=128)))

            wait("sync", S("rg%d" % par3), rg_of[(e_, 1)])
            if e_ >= 2:
                wait("sync", S("eit%d" % par3), eit_of[e_ - 2])
            for sc_ in range(2):
                eiw_of[e_] = dstep("sync", S("eiw%d" % par3),
                                   lambda e, sc_=sc_, par3=par3: e.dma_start(
                    out=eisc[par3, sc_ * 128:(sc_ + 1) * 128, :].rearrange(
                        "(a p) c -> p a c", p=128),
                    in_=land[:, par3 * 2048 + sc_ * 1024:
                             par3 * 2048 + (sc_ + 1) * 1024].rearrange(
                        "p (a c) -> p a c", c=C)))
            wait("sync", S("eiw%d" % par3), eiw_of[e_])
            if e_ >= 2:
                wait("sync", TT_, ei_used[e_ - 2])
            for ct in range(CT):
                eit_of[e_] = dstep("sync", S("eit%d" % par3),
                                   lambda e, ct=ct, par3=par3, eiTc=eiTc: e.dma_start(
                    out=eiTc[:, ct, 0:SUB],
                    in_=eisc[par3, :, ct * 128:(ct + 1) * 128], transpose=True))

            wait("scalar", S("rlgg"), rlgg_of[(e_, 1)])
            wait("scalar", TV, tokv_of[e_])
            if e_ > 1:
                wait("scalar", TV, eov_last[e_ - 2])
            for sc_ in range(2):
                nu_n = cstep("scalar", lambda e, e_=e_, sc_=sc_: e.activation(
                    wtv[:, (e_ % 2) * 2 + sc_:(e_ % 2) * 2 + sc_ + 1],
                    rlrow[:, (e_ % 2) * 2 + sc_, e_:e_ + 1], AF.Exp,
                    bias=nM128[:, e_:e_ + 1]))
            numer_of[e_] = nu_n
            wait("vector", TS, nu_n)
            wt_of[e_] = cstep("vector", lambda e, e_=e_: e.tensor_tensor(
                wtv[:, (e_ % 2) * 2:(e_ % 2) * 2 + 2],
                wtv[:, (e_ % 2) * 2:(e_ % 2) * 2 + 2],
                rd128[:, e_:e_ + 1].to_broadcast([128, 2]), OP.mult))

            eop = ((wps[0], 0, 0), (wps[1], 0, 1), (rps, 1, 0), (lps, 1, 1))

            def eo_mms(ft):
                wait("tensor", TV, h2v_of[(e_, ft)])
                if ft == 0:
                    wait("tensor", S("w3l"), w3l1_of[e_])
                    if e_ > 0:
                        wait("tensor", TV, eov_last[e_ - 1])
                if ft == 8:
                    wait("tensor", S("w3l"), w3l_of[e_])
                for ps, sc_, dc in eop:
                    cstep("tensor", lambda e, ps=ps, sc_=sc_, dc=dc, ft=ft: e.matmul(
                        ps[:, :],
                        h2b[:, ft, sc_ * 128:(sc_ + 1) * 128],
                        w3a[ft // 8][:, ft % 8, dc * 512:(dc + 1) * 512],
                        start=(ft == 0), stop=(ft == 15)))
                if ft == 7:
                    eo7_of[e_] = TT_.n

            wait("tensor", S("eit%d" % par3), eit_of[e_])
            for fb in range(4):
                par = fb % 2
                wait("tensor", S("w1l0" if par == 0 else "w1l1"), w1l_of[(e_, fb)])
                wait("tensor", S("w2l0" if par == 0 else "w2l1"), w2l_of[(e_, fb)])
                for fl in range(4):
                    ft = fb * 4 + fl
                    pk = (e_, ft - 2) if ft >= 2 else (e_ - 1, ft + 14)
                    if pk in silu3_of:
                        wait("tensor", TS, silu3_of[pk])
                    for cc in range(CT):
                        gmm_n = cstep("tensor", lambda e, par=par, fl=fl, ft=ft, cc=cc,
                                      eiTc=eiTc: e.matmul(
                            kps[ft % 2][:, 0:SUB],
                            w1a[:, cc, par * 512 + fl * 128:par * 512 + (fl + 1) * 128],
                            eiTc[:, cc, 0:SUB],
                            start=(cc == 0), stop=(cc == CT - 1)))
                    if pk in h2v_of:
                        wait("tensor", TV, h2v_of[pk])
                    for cc in range(CT):
                        vmm_n = cstep("tensor", lambda e, par=par, fl=fl, ft=ft, cc=cc,
                                      eiTc=eiTc: e.matmul(
                            vps[ft % 2][:, 0:SUB],
                            w2a[:, cc, par * 512 + fl * 128:par * 512 + (fl + 1) * 128],
                            eiTc[:, cc, 0:SUB],
                            start=(cc == 0), stop=(cc == CT - 1)))
                    if ft > 0:
                        eo_mms(ft - 1)
                    wait("scalar", TT_, gmm_n)
                    silu3_of[(e_, ft)] = cstep("scalar", lambda e, ft=ft: e.activation(
                        h1g[:, ft * SUB:(ft + 1) * SUB], kps[ft % 2][:, 0:SUB], AF.Silu))
                    wait("vector", TT_, vmm_n)
                    wait("vector", TS, silu3_of[(e_, ft)])
                    h2v_of[(e_, ft)] = cstep("vector", lambda e, ft=ft: e.tensor_tensor(
                        h2b[:, ft, :], h1g[:, ft * SUB:(ft + 1) * SUB],
                        vps[ft % 2][:, 0:SUB], OP.mult))
                g_done[(e_, fb)] = TT_.n
                if fb + 2 < 4:
                    wload(e_, fb + 2)
            ei_used[e_] = g_done[(e_, 3)]
            eo_mms(15)
            h2_used[e_] = TT_.n
            eo_done = TT_.n

            wait("vector", TT_, eo_done)
            wait("vector", TV, wt_of[e_])
            if e_ > 0:
                wait("vector", S("scat"), scat_prev)
            for ps, sc_, dc in eop:
                eov_last[e_] = cstep(
                    "vector", lambda e, ps=ps, sc_=sc_, dc=dc, e_=e_: e.tensor_scalar(
                        eob[:, sc_ * 2048 + dc * 512:sc_ * 2048 + (dc + 1) * 512],
                        ps[:, :], wtv[:, (e_ % 2) * 2 + sc_:(e_ % 2) * 2 + sc_ + 1],
                        None, OP.mult))

            scat_prev = S("scat").n
            for sc_ in range(2):
                wait("gpsimd", TV, eov_last[e_])
                wait("gpsimd", S("acw0"), S("acw0").n)
                wait("gpsimd", S("acw1"), S("acw1").n)
                wait("gpsimd", S("scat"), S("scat").n)
                dstep("gpsimd", S("scat"), lambda e, e_=e_, sc_=sc_: e.indirect_dma_start(
                    accum, IndirectOffsetOnAxis(toku[:, e_, sc_:sc_ + 1], 0),
                    eob[:, sc_ * 2048:sc_ * 2048 + 1024], None,
                    compute_op=OP.add, oob_is_err=True))
            scat_prev = S("scat").n

        wait("scalar", S("scat"), S("scat").n)
        wait("scalar", TV, TV.n)
        wait("sync", S("scat"), S("scat").n)
        wait("sync", TV, TV.n)
        stg = [kvb[:, :, :].rearrange("p a b -> p (a b)").rearrange(
                   "p (a b) -> p a b", a=4),
               kcb[:, :, :].rearrange("p a b -> p (a b)").rearrange(
                   "p (a b) -> p a b", a=4)]
        ngr = T // 512
        gld_of, gst_of = {}, {}
        wait("gpsimd", TV, TV.n)
        wait("gpsimd", S("scat"), S("scat").n)
        for g in range(ngr):
            pr = g % 2
            buf = stg[pr]
            cbase = pr * 4
            if g >= 2:
                wait("sync", S("outw%d" % pr), gst_of[g - 2])
                wait("gpsimd", S("outw%d" % pr), gst_of[g - 2])
            tl = 0
            for i in range(4):
                rt = g * 4 + i
                tl = dstep("sync" if i < 2 else "gpsimd", S("tld"),
                           lambda e, rt=rt, buf=buf, i=i:
                           e.dma_start(out=buf[:, i, :],
                                       in_=accum[rt * 128:(rt + 1) * 128, :]))
            wait("scalar", S("tld"), tl)
            wait("vector", S("tld"), tl)

            for i in range(4):
                cstep("scalar", lambda e, buf=buf, i=i, cbase=cbase: e.activation(
                    sc1[:, :], buf[:, i, :], AF.Square,
                    accum_out=lsq[:, cbase + i:cbase + i + 1]))
                lns_n = cstep("vector", lambda e, buf=buf, i=i, cbase=cbase:
                              e.tensor_scalar(
                    buf[:, i, :], buf[:, i, :], 1.0, 0.0, OP.mult, OP.add,
                    accum_out=lmn[:, cbase + i:cbase + i + 1]))
            wait("vector", TS, TS.n)
            cb4 = slice(cbase, cbase + 4)
            cstep("vector", lambda e, cb4=cb4: e.tensor_scalar(
                lrs[:, cb4], lsq[:, cb4], 1.0 / C, EPS_LN, OP.mult, OP.add))
            cstep("vector", lambda e, cb4=cb4: e.tensor_scalar(
                lmn[:, cb4], lmn[:, cb4], 1.0 / C, None, OP.mult))
            cstep("vector", lambda e, cb4=cb4: e.tensor_tensor(
                nM128[:, cb4], lmn[:, cb4], lmn[:, cb4], OP.mult))
            var_n = cstep("vector", lambda e, cb4=cb4: e.tensor_tensor(
                lrs[:, cb4], lrs[:, cb4], nM128[:, cb4], OP.subtract))
            wait("scalar", TV, var_n)
            sq_n = cstep("scalar", lambda e, cb4=cb4: e.activation(
                lrs[:, cb4], lrs[:, cb4], AF.Sqrt))
            wait("vector", TS, sq_n)
            cstep("vector", lambda e, cb4=cb4: e.reciprocal(
                lrs[:, cb4], lrs[:, cb4]))
            cstep("vector", lambda e, cb4=cb4: e.tensor_scalar(
                nM128[:, cb4], lmn[:, cb4], -1.0, None, OP.mult))
            for i in range(4):
                no_n = cstep("vector", lambda e, buf=buf, i=i, cbase=cbase:
                             e.tensor_scalar(
                    buf[:, i, :], buf[:, i, :],
                    nM128[:, cbase + i:cbase + i + 1],
                    lrs[:, cbase + i:cbase + i + 1], OP.add, OP.mult))
            wait("scalar", TV, no_n)
            for i in range(4):
                rt = g * 4 + i
                gst_of[g] = dstep("scalar", S("outw%d" % pr),
                                  lambda e, rt=rt, buf=buf, i=i: e.dma_start(
                    out=out[rt * 128:(rt + 1) * 128, :], in_=buf[:, i, :]))

        wait("sync", S("outw0"), S("outw0").n)
        wait("sync", S("outw1"), S("outw1").n)
        _emit(nc, blk, streams)
        return nc, dbg


def _emit(nc, blk, streams):
    for eng in ("sync", "scalar", "vector", "tensor", "gpsimd"):
        ops = streams[eng]

        def make(ops):
            def run(e):
                for fn in ops:
                    fn(e)
            return run
        getattr(blk, eng)(make(ops))


_NC_CACHE = {}


def kernel(**inputs):
    cs = _consts()
    if "full" not in _NC_CACHE:
        _NC_CACHE["full"] = build_program(upto="full")
    nc, _ = _NC_CACHE["full"]

    def hostv(a):
        return np.ascontiguousarray(np.asarray(a), dtype=np.float32)

    xq = hostv(inputs["L0_attn_xq"])
    base = dict(
        Wk=hostv(inputs["L0_attn_Wk"]), Wv=hostv(inputs["L0_attn_Wv"]),
        Wr=hostv(inputs["L0_attn_Wr"]), Wo=hostv(inputs["L0_attn_Wo"]),
        tmk=hostv(inputs["L0_attn_time_mix_k"]).reshape(-1),
        tmv=hostv(inputs["L0_attn_time_mix_v"]).reshape(-1),
        tmr=hostv(inputs["L0_attn_time_mix_r"]).reshape(-1),
        td=hostv(inputs["L0_attn_time_decay"]).reshape(-1),
        tfi=hostv(inputs["L0_attn_time_first"]).reshape(-1),
        rw1=hostv(inputs["L0_ffn_router_w1"]),
        rw2=hostv(inputs["L0_ffn_router_w2"]),
        w1=hostv(inputs["L0_ffn_w1"]), w2=hostv(inputs["L0_ffn_w2"]),
        w3=hostv(inputs["L0_ffn_w3"]),
    )
    for k, v in cs.items():
        base["cin_" + k] = v
    in_maps = []
    for c in range(8):
        m = dict(base)
        m["xq"] = xq[c // 2]
        in_maps.append(m)
    res = run_bass_kernel_spmd(nc, in_maps, core_ids=list(range(8)))
    outp = np.empty((B, T, C), np.float32)
    for b in range(B):
        outp[b] = res.results[2 * b]["out"]
    return outp


# version 14
# speedup vs baseline: 1.2069x; 1.2069x over previous
import numpy as np
import ml_dtypes
import concourse.bass as bass
import concourse.mybir as mybir
from concourse.bass import IndirectOffsetOnAxis
from concourse.bass_utils import run_bass_kernel_spmd
from contextlib import ExitStack

F32 = mybir.dt.float32
BF16 = mybir.dt.bfloat16
U32 = mybir.dt.uint32
OP = mybir.AluOpType
AF = mybir.ActivationFunctionType
AX = mybir.AxisListType

B, T, C, E, CAP, RH, DFF = 4, 8192, 1024, 16, 256, 128, 2048
TC = 512
NCH = T // TC
SUB = 256
SCN = 512
CT = C // 128
XOF = 16
E60 = float(np.exp(60.0))
EPS_WK = 1e-8
EPS_W = 1e-9
EPS_LN = 1e-5
BIS_ITERS = 36

bfdt = ml_dtypes.bfloat16


def _consts():
    p = np.arange(128)
    grp = p // 8
    return dict(
        ltri=(p[None, :] > p[:, None]).astype(np.float32),
        shift1=(p[None, :] == p[:, None] + 1).astype(np.float32),
        btri8=((grp[None, :] == grp[:, None]) & (p[None, :] > p[:, None])).astype(np.float32),
        sel16to128=(np.arange(16)[:, None] == grp[None, :]).astype(np.float32),
        sel128to16=(grp[:, None] == np.arange(16)[None, :]).astype(np.float32),
        identb=np.eye(128).astype(bfdt),
        identf=np.eye(128).astype(np.float32),
    )


def build_program(upto="full", nchunks=NCH):
    nc = bass.Bass("TRN2", target_bir_lowering=False, debug=False)

    def din(name, shape, dtype=F32):
        return nc.dram_tensor(name, list(shape), dtype, kind="ExternalInput").ap()

    xq = din("xq", [T, C])
    Ws = [din(n, [C, C]) for n in ("Wk", "Wv", "Wr", "Wo")]
    tmk, tmv, tmr = din("tmk", [C]), din("tmv", [C]), din("tmr", [C])
    td, tfi = din("td", [C]), din("tfi", [C])
    rw1, rw2 = din("rw1", [C, RH]), din("rw2", [RH, E])
    w1d, w2d, w3d = din("w1", [E, C, DFF]), din("w2", [E, C, DFF]), din("w3", [E, DFF, C])
    cst = {k: din("cin_" + k, v.shape, mybir.dt.from_np(v.dtype)) for k, v in _consts().items()}

    out = nc.dram_tensor("out", [T, C], F32, kind="ExternalOutput").ap()
    dbg = {}
    if upto in ("x1", "rl", "topk"):
        dbg["x1"] = nc.dram_tensor("dbg_x1", [T, C], F32, kind="ExternalOutput").ap()
    if upto in ("rl", "topk"):
        dbg["rl"] = nc.dram_tensor("dbg_rl", [E, T], F32, kind="ExternalOutput").ap()
    if upto == "topk":
        dbg["tok"] = nc.dram_tensor("dbg_tok", [128, E * 2], F32, kind="ExternalOutput").ap()
        dbg["th"] = nc.dram_tensor("dbg_th", [16, 8], F32, kind="ExternalOutput").ap()

    xbfd = nc.dram_tensor("xbfd", [T, C], BF16).ap()
    wtd = nc.dram_tensor("wtd", [4, C, C], BF16).ap()
    gsrc = nc.dram_tensor("gsrc", [T, C], BF16).ap()
    rlg = nc.dram_tensor("rlg", [T, E], F32).ap()
    rld = nc.dram_tensor("rld", [E, T], F32).ap()
    keyd = nc.dram_tensor("keyd", [E, T], F32).ap()
    idsd = nc.dram_tensor("idsd", [E, T], F32).ap()
    accum = nc.dram_tensor("accum", [T, C], F32).ap()
    eisc = nc.dram_tensor("eisc", [2, CAP, C], BF16).ap()

    streams = {k: [] for k in ("sync", "scalar", "vector", "tensor", "gpsimd")}
    full = (nchunks == NCH)

    with ExitStack() as st:
        def sbt(name, shape, dtype=F32):
            return st.enter_context(nc.sbuf_tensor(name, shape, dtype))

        wkT = sbt("wkT", [128, CT, C], BF16)
        wvT = sbt("wvT", [128, CT, C], BF16)
        wrT = sbt("wrT", [128, CT, C], BF16)
        woT = sbt("woT", [128, CT, C], BF16)
        rw1T = sbt("rw1T", [128, CT, RH], BF16)
        rw2b = sbt("rw2b", [128, E], BF16)
        vtmk, vtmv, vtmr = sbt("vtmk", [128, CT]), sbt("vtmv", [128, CT]), sbt("vtmr", [128, CT])
        vdec, vtf = sbt("vdec", [128, CT]), sbt("vtf", [128, CT])
        ones1 = sbt("ones1", [1, 128])
        c_ltri, c_shift = sbt("c_ltri", [128, 128]), sbt("c_shift", [128, 128])
        c_btri = sbt("c_btri", [128, 128])
        c_s16, c_s128 = sbt("c_s16", [16, 128]), sbt("c_s128", [128, 16])
        c_idf = sbt("c_idf", [128, 128])

        xT = sbt("xT", [128, 2, CT, TC + XOF], BF16)
        xm = sbt("xm", [128, 2, CT, TC], BF16)
        xmB = sbt("xmB", [128, CT, TC], BF16)
        kvb = sbt("kvb", [128, CT, TC])
        kcb = sbt("kcb", [128, CT, TC])
        ybuf = sbt("ybuf", [128, CT, SCN + 1])
        ckv = sbt("ckv", [128, CT])
        ckc = sbt("ckc", [128, CT])
        srb = sbt("srb", [128, CT, TC], BF16)
        oTb = sbt("oTb", [128, CT, TC], BF16)
        xrow = sbt("xrow", [128, 2, C])
        x1p = sbt("x1p", [128, 2, C])
        x1T = sbt("x1T", [128, CT, TC], BF16)
        hidT = sbt("hidT", [128, TC], BF16)
        rlc = sbt("rlc", [16, TC])
        rlt = sbt("rlt", [128, 4, 16])
        lsq, lmn, lrs = sbt("lsq", [128, 8]), sbt("lmn", [128, 8]), sbt("lrs", [128, 8])
        Mrun, mrun = sbt("Mrun", [16, 1]), sbt("mrun", [16, 1])
        t16a, t16b = sbt("t16a", [16, 1]), sbt("t16b", [16, 1])
        lo16, hi16, mid16 = sbt("lo16", [16, 1]), sbt("hi16", [16, 1]), sbt("mid16", [16, 1])
        q16, z16, dn16 = sbt("q16", [16, 1]), sbt("z16", [16, 1]), sbt("dn16", [16, 1])
        cnt16, n116 = sbt("cnt16", [16, 1]), sbt("n116", [16, 1])
        row16 = sbt("row16", [1, 16])
        cntp = sbt("cntp", [128, 1])
        r1, r2, r3, r4 = sbt("r1", [128, 1]), sbt("r2", [128, 1]), sbt("r3", [128, 1]), sbt("r4", [128, 1])
        c_epswk = sbt("c_epswk", [128, 1])
        c_one1 = sbt("c_one1", [128, 1])
        nM128, rd128 = sbt("nM128", [128, 16]), sbt("rd128", [128, 16])
        sc1 = sbt("sc1", [128, 1024])
        keyt = sbt("keyt", [128, 64])
        keytab = sbt("keytab", [128, 64])
        idst = sbt("idst", [128, 64])
        iotaP = sbt("iotaP", [128, 1])
        pb64 = sbt("pb64", [128, 1])
        pb64b = sbt("pb64b", [128, 1])
        iotaS = sc1[:, 0:256]
        Itile = sc1[:, 256:512]
        toks = sbt("toks", [128, E, 2])
        toku = sbt("toku", [128, E, 2], U32)
        flatu = sbt("flatu", [128, E, 2], U32)
        rlrow = sbt("rlrow", [128, 4, 16])
        wtv = sbt("wtv", [128, 4])
        mge = sbt("mge", [16, 1], U32)
        mlt = sbt("mlt", [16, 1], U32)

        rls = srb[:, :, :].rearrange("p a b -> p (a b)").bitcast(F32)[:, 0:1024]
        selb = oTb[:, :, :].rearrange("p a b -> p (a b)").bitcast(F32)[:, 0:1024]
        gathf = xm[:, 0, :, :].rearrange("p a b -> p (a b)")
        eiT = xm[:, 1, :, :]
        h1g = xT[:, 0, :, :].rearrange("p a b -> p (a b)")
        h2b = xT[:, 1, :, :].rearrange("p a b -> p (a b)")[:, 0:4096].rearrange(
            "p (a b) -> p a b", a=16)
        eob = kvb[:, :, :].rearrange("p a b -> p (a b)")
        w3a = (wkT, wvT)
        w1a = wrT
        w2a = woT

        kps = [st.enter_context(nc.psum_tensor(f"kps{i}", [128, TC], F32)) for i in range(2)]
        vps = [st.enter_context(nc.psum_tensor(f"vps{i}", [128, TC], F32)) for i in range(2)]
        rps = st.enter_context(nc.psum_tensor("rps", [128, TC], F32))
        wps = [st.enter_context(nc.psum_tensor(f"wps{i}", [128, TC], F32)) for i in range(2)]
        lps = st.enter_context(nc.psum_tensor("lps", [128, 512], F32))

        sems = {}

        class Sem:
            __slots__ = ("h", "n")

            def __init__(self, name):
                self.h = st.enter_context(nc.semaphore(name))
                self.n = 0

        def S(name):
            if name not in sems:
                sems[name] = Sem("s" + name)
            return sems[name]

        tick = {e: S("tick_" + e) for e in ("vector", "scalar", "tensor", "gpsimd")}
        TV, TS, TT_, TG = tick["vector"], tick["scalar"], tick["tensor"], tick["gpsimd"]

        blk = st.enter_context(nc.Block())

        def on(eng, fn):
            streams[eng].append(fn)

        def wait(eng, sem, val=None):
            v = sem.n if val is None else val
            if v > 0:
                on(eng, lambda e, s=sem, v=v: e.wait_ge(s.h, v))

        def dstep(eng, sem, fn):
            def go(e, s=sem, fn=fn):
                fn(e).then_inc(s.h, 16)
            on(eng, go)
            sem.n += 16
            return sem.n

        def cstep(eng, fn, selfwait=True):
            t = tick[eng]
            if selfwait and eng != "tensor" and t.n > 0:
                on(eng, lambda e, s=t, v=t.n: e.wait_ge(s.h, v))

            def go(e, s=t, fn=fn):
                fn(e).then_inc(s.h, 1)
            on(eng, go)
            t.n += 1
            return t.n

        def bc(vec, n, ct=None):
            if ct is None:
                return vec[:, :].unsqueeze(2).to_broadcast([128, CT, n])
            return vec[:, ct:ct + 1].to_broadcast([128, n])

        for i, W in enumerate(Ws):
            wait("gpsimd", S("wcast"), 16 * i)
            dstep("gpsimd", S("wcast"), lambda e, W=W, i=i: e.dma_start(out=wtd[i], in_=W))
        for j in range(nchunks):
            wait("gpsimd", S("xbf"), 16 * j)
            dstep("gpsimd", S("xbf"), lambda e, j=j: e.dma_start(
                out=xbfd[j * TC:(j + 1) * TC, :], in_=xq[j * TC:(j + 1) * TC, :]))
        dstep("gpsimd", S("gset"), lambda e: e.dma_start(
            out=rw1T[:, :, :], in_=rw1.rearrange("(a p) h -> p a h", p=128)))
        dstep("gpsimd", S("gset"), lambda e: e.dma_start(out=rw2b[:, :], in_=rw2))
        cstep("gpsimd", lambda e: e.iota(iotaP[:, :], [[1, 1]], base=0,
                                         channel_multiplier=1,
                                         allow_small_or_imprecise_dtypes=True))
        cstep("gpsimd", lambda e: e.iota(pb64[:, :], [[1, 1]], base=0,
                                         channel_multiplier=64,
                                         allow_small_or_imprecise_dtypes=True))
        cstep("gpsimd", lambda e: e.iota(pb64b[:, :], [[1, 1]], base=64,
                                         channel_multiplier=64,
                                         allow_small_or_imprecise_dtypes=True))

        wait("sync", S("wcast"))
        for i, wT in enumerate((wkT, wvT, wrT, woT)):
            for ct in range(CT):
                dstep("sync", S("set"), lambda e, i=i, ct=ct, wT=wT: e.dma_start(
                    out=wT[:, ct, :], in_=wtd[i, :, ct * 128:(ct + 1) * 128], transpose=True))
        for src, dst in ((tmk, vtmk), (tmv, vtmv), (tmr, vtmr), (td, vdec), (tfi, vtf)):
            dstep("sync", S("set"), lambda e, src=src, dst=dst: e.dma_start(
                out=dst[:, :], in_=src.rearrange("(a p) -> p a", p=128),
                allow_slow_non_contiguous=True))
        for name, dst in (("ltri", c_ltri), ("shift1", c_shift), ("btri8", c_btri),
                          ("sel16to128", c_s16), ("sel128to16", c_s128),
                          ("identf", c_idf)):
            dstep("sync", S("set"), lambda e, name=name, dst=dst: e.dma_start(
                out=dst[:, :], in_=cst[name]))
        set_n = S("set").n

        zn = cstep("vector", lambda e: e.memset(sc1[:, :], 0.0), selfwait=False)
        wait("sync", TV, zn)
        dstep("sync", S("set"), lambda e: e.dma_start(
            out=idsd.rearrange("e t -> (e t)").rearrange(
                "(p c) -> p c", c=1024), in_=sc1[:, :]))
        for zi in range(4):
            dstep("sync", S("set"), lambda e, zi=zi: e.dma_start(
                out=eisc[zi // 2, (zi % 2) * 128:(zi % 2) * 128 + 128, :],
                in_=sc1[:, :].bitcast(BF16)[:, 0:1024]))

        wait("scalar", S("set"), S("set").n)
        cstep("scalar", lambda e: e.activation(vdec[:, :], vdec[:, :], AF.Exp))
        cstep("scalar", lambda e: e.activation(vdec[:, :], vdec[:, :], AF.Exp, scale=-1.0))
        ts_setup = cstep("scalar", lambda e: e.activation(vtf[:, :], vtf[:, :], AF.Exp))
        wait("vector", S("set"), S("set").n)
        cstep("vector", lambda e: e.memset(ones1[:, :], 1.0))
        cstep("vector", lambda e: e.memset(c_epswk[:, :], EPS_WK))
        cstep("vector", lambda e: e.memset(c_one1[:, :], 1.0))
        cstep("vector", lambda e: e.memset(ckv[:, :], 0.0))
        cstep("vector", lambda e: e.memset(ckc[:, :], 0.0))
        cstep("vector", lambda e: e.memset(xT[:, 1, :, XOF + TC - 1:XOF + TC], 0.0))
        cstep("vector", lambda e: e.memset(Mrun[:, :], -1e30))
        tv_setup = cstep("vector", lambda e: e.memset(mrun[:, :], 1e30))

        wait("tensor", S("set"), S("set").n)
        wait("tensor", S("gset"), S("gset").n)

        kdone, vdone, rdone = [], [], []
        kcS_of, srS_of = {}, {}
        kv_of = {}
        wfree = {}
        s10mm, s10_of, x1v_of = {}, {}, {}
        odone, hid_of, slu_of, rlm_of = [], [], [], []
        rtp_of, rlv_of, rtv_of = [], [], []
        gsw_of, acw_of, dbgx_of = {}, {}, {}
        rlw_of, rtw_of, dbgr_of = [], [], []
        xrw_of = {}
        oV_of = []
        premix_bnd, premix_done, premixG_done, rec_of = {}, {}, {}, {}
        oG_of, mult_of = {}, {}

        def xpose(j):
            rj = j % 2
            t0 = j * TC
            xts = S("xt0") if rj == 0 else S("xt1")
            wait("sync", S("xbf"), 16 * (j + 1))
            if j >= 1:
                wait("sync", TV, premix_bnd[j - 1])
            for ct in range(CT):
                dstep("sync", xts, lambda e, ct=ct, rj=rj, t0=t0: e.dma_start(
                    out=xT[:, rj, ct, XOF:XOF + TC],
                    in_=xbfd[t0:t0 + TC, ct * 128:(ct + 1) * 128], transpose=True))

        def premix(j):
            rj = j % 2
            xts = S("xt0") if rj == 0 else S("xt1")
            wait("vector", xts)
            if j == 0:
                wait("vector", TV, tv_setup)
            bnd_n = cstep("vector", lambda e, rj=rj: e.tensor_copy(
                xT[:, rj, :, XOF - 1:XOF], xT[:, 1 - rj, :, XOF + TC - 1:XOF + TC]))
            premix_bnd[j] = bnd_n
            wait("gpsimd", xts)
            wait("gpsimd", TV, bnd_n)
            dif_n = cstep("gpsimd", lambda e, rj=rj: e.tensor_tensor(
                xm[:, 1, :, :], xT[:, rj, :, XOF:XOF + TC],
                xT[:, rj, :, XOF - 1:XOF + TC - 1], OP.subtract))
            wait("vector", TG, dif_n)
            if j > 0:
                wait("vector", TT_, rdone[j - 1])
            for ct in range(CT):
                cstep("vector", lambda e, rj=rj, ct=ct: e.scalar_tensor_tensor(
                    xm[:, 0, ct, :], xm[:, 1, ct, :], vtmk[:, ct:ct + 1],
                    xT[:, rj, ct, XOF - 1:XOF + TC - 1], OP.mult, OP.add))
            if j > 0:
                wait("vector", TT_, vdone[j - 1])
            for ct in range(CT):
                cstep("vector", lambda e, rj=rj, ct=ct: e.scalar_tensor_tensor(
                    xmB[:, ct, :], xm[:, 1, ct, :], vtmv[:, ct:ct + 1],
                    xT[:, rj, ct, XOF - 1:XOF + TC - 1], OP.mult, OP.add))
            premix_done[j] = TV.n

        def fmain(j):
            rj = j % 2
            wait("tensor", TV, premix_done[j])
            for dt in range(CT):
                fk = (j, dt - 2) if dt >= 2 else (j - 1, dt + 6)
                if fk in kcS_of:
                    wait("tensor", TS, kcS_of[fk])
                for cc in range(CT):
                    kn = cstep("tensor", lambda e, dt=dt, cc=cc: e.matmul(
                        kps[dt % 2][:, :], wkT[:, cc, dt * 128:(dt + 1) * 128],
                        xm[:, 0, cc, :], start=(cc == 0), stop=(cc == CT - 1)))
                wait("scalar", TT_, kn)
                if j == 0 and dt == 0:
                    wait("scalar", TS, ts_setup)
                kcS_of[(j, dt)] = cstep("scalar", lambda e, dt=dt: e.activation(
                    kcb[:, dt, :], kps[dt % 2][:, :], AF.Exp))
            kdone.append(TT_.n)

            for dt in range(CT):
                fk = (j, dt - 2) if dt >= 2 else (j - 1, dt + 6)
                if fk in kv_of:
                    wait("tensor", TV, kv_of[fk])
                for cc in range(CT):
                    vn = cstep("tensor", lambda e, dt=dt, cc=cc: e.matmul(
                        vps[dt % 2][:, :], wvT[:, cc, dt * 128:(dt + 1) * 128],
                        xmB[:, cc, :], start=(cc == 0), stop=(cc == CT - 1)))
                wait("vector", TS, kcS_of[(j, dt)])
                cstep("vector", lambda e, dt=dt: e.tensor_scalar_min(
                    kcb[:, dt, :], kcb[:, dt, :], E60))
                wait("vector", TT_, vn)
                kv_of[(j, dt)] = cstep("vector", lambda e, dt=dt: e.tensor_tensor(
                    kvb[:, dt, :], kcb[:, dt, :], vps[dt % 2][:, :], OP.mult))
            vdone.append(TT_.n)

            wait("vector", TT_, kdone[j])
            for ct in range(CT):
                mixR_n = cstep("vector", lambda e, rj=rj, ct=ct: e.scalar_tensor_tensor(
                    xm[:, 0, ct, :], xm[:, 1, ct, :], vtmr[:, ct:ct + 1],
                    xT[:, rj, ct, XOF - 1:XOF + TC - 1], OP.mult, OP.add))

            wait("tensor", TV, mixR_n)
            for dt in range(CT):
                fk = (j, dt - 1) if dt >= 1 else (j - 1, 7)
                if fk in srS_of:
                    wait("tensor", TS, srS_of[fk])
                for cc in range(CT):
                    rn = cstep("tensor", lambda e, dt=dt, cc=cc: e.matmul(
                        rps[:, :], wrT[:, cc, dt * 128:(dt + 1) * 128],
                        xm[:, 0, cc, :], start=(cc == 0), stop=(cc == CT - 1)))
                wait("scalar", TT_, rn)
                srS_of[(j, dt)] = cstep("scalar", lambda e, dt=dt: e.activation(
                    srb[:, dt, :], rps[:, :], AF.Sigmoid))
            rdone.append(TT_.n)

            for si, (src, car, res) in enumerate(((kvb, ckv, kvb), (kcb, ckc, kcb))):
                cstep("vector", lambda e, car=car: e.tensor_copy(
                    ybuf[:, :, 0:1], car[:, :].unsqueeze(2)))
                for dt in range(CT):
                    if si == 0:
                        wait("vector", TV, kv_of[(j, dt)])
                    cstep("vector", lambda e, dt=dt, src=src:
                          e.tensor_tensor_scan(
                              ybuf[:, dt, 1:SCN + 1], bc(vdec, SCN, dt),
                              src[:, dt, 0:SCN], ybuf[:, dt, 0:1],
                              OP.mult, OP.add))
                cstep("vector", lambda e, car=car: e.tensor_copy(
                    car[:, :].unsqueeze(2), ybuf[:, :, SCN:SCN + 1]))
                for dt in range(CT):
                    wk_n = cstep("vector", lambda e, dt=dt, res=res:
                                 e.scalar_tensor_tensor(
                        res[:, dt, :], res[:, dt, :], vtf[:, dt:dt + 1],
                        ybuf[:, dt, 0:SCN], OP.mult, OP.add))
            wait("scalar", TV, wk_n)
            cstep("scalar", lambda e: e.activation(
                ybuf[:, :, 0:SCN], kcb[:, :, :], AF.Ln, bias=c_epswk[:, :]))
            rec_of[j] = cstep("scalar", lambda e: e.activation(
                ybuf[:, :, 0:SCN], ybuf[:, :, 0:SCN], AF.Exp, scale=-1.0))

        def ftail(j):
            wait("vector", TS, rec_of[j])
            mult_of[j] = cstep("vector", lambda e: e.tensor_tensor(
                kvb[:, :, :], kvb[:, :, :], ybuf[:, :, 0:SCN], OP.mult))
            wait("vector", TS, srS_of[(j, CT - 1)])
            if j > 0:
                wait("vector", TT_, odone[j - 1])
            cstep("vector", lambda e: e.tensor_tensor(
                oTb[:, :, :], kvb[:, :, :], srb[:, :, :], OP.mult))
            oV_of.append(TV.n)

        def back(j):
            t0 = j * TC
            wait("tensor", TV, oV_of[j])
            if j > 0:
                wait("tensor", TS, slu_of[j - 1])
            for h in range(2):
                ta = t0 + h * 256
                prev = (j, 1) if h == 1 else (j - 1, 3)
                if prev in s10_of:
                    wait("scalar", TV, s10_of[prev])
                xrw_of[(j, h)] = dstep("scalar", S("xrw"), lambda e, ta=ta: e.dma_start(
                    out=xrow[:, :, :], in_=xq[ta:ta + 256, :].rearrange(
                        "(a p) c -> p a c", p=128)))
                for hi in range(2):
                    sub = 2 * h + hi
                    for dc in range(2):
                        fk = (j, sub - 1, dc) if sub >= 1 else (j - 1, 3, dc)
                        if fk in wfree:
                            wait("tensor", TV, wfree[fk])
                        for cc in range(CT):
                            cstep("tensor", lambda e, sub=sub, dc=dc, cc=cc: e.matmul(
                                wps[dc][:, :], oTb[:, cc, sub * 128:(sub + 1) * 128],
                                woT[:, cc, dc * 512:(dc + 1) * 512],
                                start=(cc == 0), stop=(cc == CT - 1)))
                    s10mm[(j, sub)] = TT_.n
                    wait("vector", S("xrw"), xrw_of[(j, h)])
                    wait("vector", TT_, s10mm[(j, sub)])
                    prevs = (j, sub - 2) if sub >= 2 else (j - 1, sub + 2)
                    if prevs in gsw_of:
                        wait("vector", S("gsw%d" % (sub % 2)), gsw_of[prevs])
                        wait("vector", S("acw%d" % (sub % 2)), acw_of[prevs])
                        if "x1" in dbg:
                            wait("vector", S("dbgx%d" % (sub % 2)), dbgx_of[prevs])
                    for dc in range(2):
                        wfree[(j, sub, dc)] = cstep(
                            "vector", lambda e, hi=hi, dc=dc: e.tensor_tensor(
                                x1p[:, hi, dc * 512:(dc + 1) * 512], wps[dc][:, :],
                                xrow[:, hi, dc * 512:(dc + 1) * 512], OP.add))
                    s10_of[(j, sub)] = TV.n
                    wait("scalar", TV, s10_of[(j, sub)])
                    cstep("scalar", lambda e, hi=hi: e.activation(
                        sc1[:, :], x1p[:, hi, :], AF.Square, accum_out=lsq[:, hi:hi + 1]))
                    lns_n = cstep("scalar", lambda e, hi=hi: e.activation(
                        sc1[:, :], x1p[:, hi, :], AF.Identity, scale=1.0 / C,
                        accum_out=lmn[:, hi:hi + 1]))
                    wait("vector", TS, lns_n)
                    cstep("vector", lambda e, hi=hi: e.tensor_scalar(
                        lrs[:, hi:hi + 1], lsq[:, hi:hi + 1], 1.0 / C, None, OP.mult))
                    cstep("vector", lambda e, hi=hi: e.tensor_tensor(
                        r1[:, 0:1], lmn[:, hi:hi + 1], lmn[:, hi:hi + 1], OP.mult))
                    cstep("vector", lambda e, hi=hi: e.tensor_tensor(
                        lrs[:, hi:hi + 1], lrs[:, hi:hi + 1], r1[:, 0:1], OP.subtract))
                    var_n = cstep("vector", lambda e, hi=hi: e.tensor_scalar(
                        lrs[:, hi:hi + 1], lrs[:, hi:hi + 1], EPS_LN, None, OP.add))
                    wait("scalar", TV, var_n)
                    sqt_n = cstep("scalar", lambda e, hi=hi: e.activation(
                        lrs[:, hi:hi + 1], lrs[:, hi:hi + 1], AF.Sqrt))
                    wait("vector", TS, sqt_n)
                    cstep("vector", lambda e, hi=hi: e.reciprocal(
                        lrs[:, hi:hi + 1], lrs[:, hi:hi + 1]))
                    cstep("vector", lambda e, hi=hi: e.tensor_scalar(
                        r2[:, 0:1], lmn[:, hi:hi + 1], -1.0, None, OP.mult))
                    x1v_of[(j, sub)] = cstep("vector", lambda e, hi=hi: e.tensor_scalar(
                        x1p[:, hi, :], x1p[:, hi, :], r2[:, 0:1], lrs[:, hi:hi + 1],
                        OP.add, OP.mult))
                    wait("sync", TV, x1v_of[(j, sub)])
                    acw_of[(j, sub)] = dstep("sync", S("acw%d" % (sub % 2)),
                                             lambda e, ta=ta, hi=hi: e.dma_start(
                        out=accum[ta + hi * 128:ta + (hi + 1) * 128, :],
                        in_=x1p[:, hi, :]))
                    wait("gpsimd", TV, x1v_of[(j, sub)])
                    gsw_of[(j, sub)] = dstep("gpsimd", S("gsw%d" % (sub % 2)),
                                             lambda e, ta=ta, hi=hi: e.dma_start(
                        out=gsrc[ta + hi * 128:ta + (hi + 1) * 128, :],
                        in_=x1p[:, hi, :]))
                    if "x1" in dbg:
                        dbgx_of[(j, sub)] = dstep("sync", S("dbgx%d" % (sub % 2)),
                                                  lambda e, ta=ta, hi=hi: e.dma_start(
                            out=dbg["x1"][ta + hi * 128:ta + (hi + 1) * 128, :],
                            in_=x1p[:, hi, :]))

            odone.append(s10mm[(j, 3)])

            wait("sync", S("gsw0"), gsw_of[(j, 2)])
            wait("sync", S("gsw1"), gsw_of[(j, 3)])
            if j > 0:
                wait("sync", TT_, hid_of[j - 1])
            for ct in range(CT):
                dstep("sync", S("x1t"), lambda e, ct=ct, t0=t0: e.dma_start(
                    out=x1T[:, ct, :], in_=gsrc[t0:t0 + TC, ct * 128:(ct + 1) * 128],
                    transpose=True))
            wait("tensor", S("x1t"))
            if j == 0:
                wait("tensor", S("gset"), 32)
            wait("tensor", TV, wfree[(j, 3, 0)])
            for cc in range(CT):
                hn = cstep("tensor", lambda e, cc=cc: e.matmul(
                    wps[0][:, :], rw1T[:, cc, :], x1T[:, cc, :],
                    start=(cc == 0), stop=(cc == CT - 1)))
            hid_of.append(hn)
            wait("scalar", TT_, hn)
            if j > 0:
                wait("scalar", TT_, rlm_of[j - 1])
            slu_of.append(cstep("scalar", lambda e: e.activation(
                hidT[:, :], wps[0][:, :], AF.Silu)))

        def backB(j):
            t0 = j * TC
            wait("tensor", TS, slu_of[j])
            if j > 0:
                wait("tensor", TV, rtv_of[j - 1])
            rlm_of.append(cstep("tensor", lambda e: e.matmul(
                lps[0:16, :], rw2b[:, :], hidT[:, :], start=True, stop=True)))
            wait("vector", TT_, rlm_of[j])
            if j > 0:
                wait("vector", S("rlw"), rlw_of[j - 1])
                wait("vector", TT_, rtp_of[j - 1])
                if "rl" in dbg:
                    wait("vector", S("dbgr"), dbgr_of[j - 1])
            rlv_of.append(cstep("vector", lambda e: e.tensor_copy(
                rlc[:, :], lps[0:16, :])))
            cstep("vector", lambda e: e.reduce_max(t16a[:, :], rlc[:, :], axis=AX.X))
            cstep("vector", lambda e: e.tensor_max(Mrun[:, :], Mrun[:, :], t16a[:, :]))
            cstep("vector", lambda e: e.tensor_reduce(t16b[:, :], rlc[:, :], AX.X, OP.min))
            mm_n = cstep("vector", lambda e: e.tensor_tensor(
                mrun[:, :], mrun[:, :], t16b[:, :], OP.min))
            wait("sync", TV, mm_n)
            rlw_of.append(dstep("sync", S("rlw"), lambda e, t0=t0: e.dma_start(
                out=rld[:, t0:t0 + TC], in_=rlc[:, :])))
            if "rl" in dbg:
                dbgr_of.append(dstep("sync", S("dbgr"), lambda e, t0=t0: e.dma_start(
                    out=dbg["rl"][:, t0:t0 + TC], in_=rlc[:, :])))
            wait("tensor", TV, rlv_of[j])
            for a_ in range(4):
                rt_n = cstep("tensor", lambda e, a_=a_: e.transpose(
                    lps[:, a_ * 16:(a_ + 1) * 16],
                    rlc[:, a_ * 128:(a_ + 1) * 128], c_idf[0:16, 0:16]))
            rtp_of.append(rt_n)
            wait("vector", TT_, rt_n)
            if j > 0:
                wait("vector", S("rtw"), rtw_of[j - 1])
            rtv_of.append(cstep("vector", lambda e: e.tensor_copy(
                rlt[:, :, :], lps[:, 0:64].rearrange("p (a b) -> p a b", a=4))))
            wait("sync", TV, rtv_of[j])
            rtw_of.append(dstep("sync", S("rtw"), lambda e, t0=t0: e.dma_start(
                out=rlg[t0:t0 + TC, :].rearrange("(a p) e -> p a e", p=128),
                in_=rlt[:, :, :])))

        xpose(0)
        premix(0)
        if nchunks > 1:
            xpose(1)
        for j in range(nchunks):
            fmain(j)
            if j + 1 < nchunks:
                premix(j + 1)
            if j > 1:
                backB(j - 2)
            if j > 0:
                back(j - 1)
            ftail(j)
            if j + 2 < nchunks:
                xpose(j + 2)
        backB(nchunks - 2)
        back(nchunks - 1)
        backB(nchunks - 1)

        if upto == "x1" or not full:
            for sname in ("dbgx0", "dbgx1", "acw0", "acw1", "gsw0", "gsw1",
                          "rtw", "rlw", "dbgr"):
                wait("sync", S(sname))
            _emit(nc, blk, streams)
            return nc, dbg

        ph1_tt = TT_.n
        w1l_of, w2l_of, w3l_of, w3l1_of = {}, {}, {}, {}
        g_done = {}

        def wload(e_, fb):
            par = fb % 2
            pk = (e_, fb - 2) if fb >= 2 else (e_ - 1, fb + 2)
            if pk in g_done:
                wait("gpsimd", TT_, g_done[pk])
            w1l_of[(e_, fb)] = dstep("gpsimd", S("w1l0" if par == 0 else "w1l1"),
                                     lambda e, e_=e_, fb=fb, par=par: e.dma_start(
                out=w1a[:, :, par * 512:(par + 1) * 512],
                in_=w1d[e_, :, fb * 512:(fb + 1) * 512].rearrange(
                    "(a p) f -> p a f", p=128)))
            w2l_of[(e_, fb)] = dstep("gpsimd", S("w2l0" if par == 0 else "w2l1"),
                                     lambda e, e_=e_, fb=fb, par=par: e.dma_start(
                out=w2a[:, :, par * 512:(par + 1) * 512],
                in_=w2d[e_, :, fb * 512:(fb + 1) * 512].rearrange(
                    "(a p) f -> p a f", p=128)))

        wait("gpsimd", TT_, ph1_tt)
        w3l1_of[0] = dstep("gpsimd", S("w3l"), lambda e: e.dma_start(
            out=wkT[:, :, :],
            in_=w3d[0, 0:1024, :].rearrange("(a p) d -> p a d", p=128)))
        w3l_of[0] = dstep("gpsimd", S("w3l"), lambda e: e.dma_start(
            out=wvT[:, :, :],
            in_=w3d[0, 1024:2048, :].rearrange("(a p) d -> p a d", p=128)))
        wload(0, 0)
        wload(0, 1)

        wait("sync", S("rlw"), rlw_of[-1])
        wait("sync", TV, TV.n)
        rlsl_n = dstep("sync", S("rlsl"), lambda e: e.dma_start(
            out=rls, in_=rld.rearrange("e (s t) -> (e s) t", s=8)))
        wait("vector", S("rlsl"), rlsl_n)

        neg_n = cstep("vector", lambda e: e.tensor_scalar(
            t16a[:, :], Mrun[:, :], -1.0, None, OP.mult))
        wait("tensor", TV, max(neg_n, rlv_of[-1], rtv_of[-1]))
        wait("tensor", TS, kcS_of[(nchunks - 1, CT - 1)])
        bN = cstep("tensor", lambda e: e.matmul(
            kps[0][:, 0:1], c_s16[:, :], t16a[:, :], start=True, stop=True))
        wait("vector", TT_, bN)
        r1_n = cstep("vector", lambda e: e.tensor_copy(r1[:, 0:1], kps[0][:, 0:1]))

        wait("scalar", TV, r1_n)
        z1 = cstep("scalar", lambda e: e.activation(
            sc1[:, :], rls, AF.Exp, bias=r1[:, 0:1], accum_out=cntp[:, 0:1]))
        wait("tensor", TS, z1)
        z2 = cstep("tensor", lambda e: e.matmul(
            kps[0][0:16, 0:1], c_s128[:, :], cntp[:, 0:1], start=True, stop=True))
        wait("vector", TT_, z2)
        cstep("vector", lambda e: e.tensor_copy(z16[:, :], kps[0][0:16, 0:1]))

        wait("vector", TS, z1)
        cstep("vector", lambda e: e.tensor_scalar(
            lo16[:, :], mrun[:, :], -1.0, None, OP.add))
        cstep("vector", lambda e: e.tensor_copy(hi16[:, :], Mrun[:, :]))
        for it in range(BIS_ITERS):
            m_n = cstep("vector", lambda e: e.tensor_scalar(
                mid16[:, :], lo16[:, :], hi16[:, 0:1], 0.5, OP.add, OP.mult))
            wait("tensor", TV, m_n)
            b_n = cstep("tensor", lambda e: e.matmul(
                kps[0][:, 0:1], c_s16[:, :], mid16[:, :], start=True, stop=True))
            wait("vector", TT_, b_n)
            cstep("vector", lambda e: e.tensor_copy(r2[:, 0:1], kps[0][:, 0:1]))
            c_n = cstep("vector", lambda e: e.tensor_scalar(
                sc1[:, :], rls, r2[:, 0:1], 0.0, OP.is_gt, OP.add,
                accum_out=cntp[:, 0:1]))
            wait("tensor", TV, c_n)
            cm = cstep("tensor", lambda e: e.matmul(
                kps[0][0:16, 0:1], c_s128[:, :], cntp[:, 0:1], start=True, stop=True))
            wait("vector", TT_, cm)
            cstep("vector", lambda e: e.tensor_scalar(
                mge[:, :], kps[0][0:16, 0:1], float(CAP), None, OP.is_ge))
            cstep("vector", lambda e: e.select(
                lo16[:, :], mge[:, :], mid16[:, :], lo16[:, :]))
            cstep("vector", lambda e: e.select(
                hi16[:, :], mge[:, :], hi16[:, :], mid16[:, :]))
        th_n = cstep("vector", lambda e: e.tensor_copy(mid16[:, :], hi16[:, :]))
        wait("tensor", TV, th_n)
        b_n = cstep("tensor", lambda e: e.matmul(
            kps[0][:, 0:1], c_s16[:, :], mid16[:, :], start=True, stop=True))
        wait("vector", TT_, b_n)
        cstep("vector", lambda e: e.tensor_copy(r2[:, 0:1], kps[0][:, 0:1]))
        c_n = cstep("vector", lambda e: e.tensor_scalar(
            sc1[:, :], rls, r2[:, 0:1], 0.0, OP.is_gt, OP.add, accum_out=cntp[:, 0:1]))
        wait("tensor", TV, c_n)
        cm = cstep("tensor", lambda e: e.matmul(
            kps[0][0:16, 0:1], c_s128[:, :], cntp[:, 0:1], start=True, stop=True))
        wait("vector", TT_, cm)
        cstep("vector", lambda e: e.tensor_copy(n116[:, :], kps[0][0:16, 0:1]))
        q_n = cstep("vector", lambda e: e.tensor_scalar(
            q16[:, :], n116[:, :], -1.0, float(CAP), OP.mult, OP.add))

        wait("vector", TT_, odone[-1])
        cstep("vector", lambda e: e.tensor_scalar(
            selb, rls, r2[:, 0:1], None, OP.is_gt))
        tm_n = cstep("vector", lambda e: e.tensor_scalar(
            sc1[:, :], rls, r2[:, 0:1], 0.0, OP.is_equal, OP.add,
            accum_out=cntp[:, 0:1]))
        cstep("vector", lambda e: e.tensor_tensor_scan(
            sc1[:, :], c_one1[:, 0:1].to_broadcast([128, 1024]), sc1[:, :],
            0.0, OP.mult, OP.add))
        wait("tensor", TV, tm_n)
        ob_n = cstep("tensor", lambda e: e.matmul(
            kps[1][:, 0:1], c_btri[:, :], cntp[:, 0:1], start=True, stop=True))
        wait("vector", TT_, ob_n)
        cstep("vector", lambda e: e.tensor_copy(r3[:, 0:1], kps[1][:, 0:1]))
        cstep("vector", lambda e: e.tensor_scalar(
            sc1[:, :], sc1[:, :], r3[:, 0:1], None, OP.add))
        qb_n = cstep("vector", lambda e: e.tensor_copy(t16a[:, :], q16[:, :]))
        wait("tensor", TV, qb_n)
        qb2 = cstep("tensor", lambda e: e.matmul(
            kps[1][:, 0:1], c_s16[:, :], t16a[:, :], start=True, stop=True))
        wait("vector", TT_, qb2)
        cstep("vector", lambda e: e.tensor_copy(r4[:, 0:1], kps[1][:, 0:1]))
        cstep("vector", lambda e: e.tensor_scalar(
            sc1[:, :], sc1[:, :], r4[:, 0:1], None, OP.is_le))
        cstep("vector", lambda e: e.scalar_tensor_tensor(
            sc1[:, :], rls, r2[:, 0:1], sc1[:, :], OP.is_equal, OP.mult))
        sel_n = cstep("vector", lambda e: e.tensor_tensor(selb, selb, sc1[:, :], OP.add))

        wait("scalar", TV, sel_n)
        es_n = cstep("scalar", lambda e: e.activation(
            sc1[:, :], rls, AF.Exp, bias=r1[:, 0:1]))
        wait("vector", TS, es_n)
        cstep("vector", lambda e: e.tensor_tensor(sc1[:, :], sc1[:, :], selb, OP.mult))
        d_n = cstep("vector", lambda e: e.tensor_scalar(
            sc1[:, :], sc1[:, :], 1.0, 0.0, OP.mult, OP.add, accum_out=cntp[:, 0:1]))
        wait("tensor", TV, d_n)
        dm = cstep("tensor", lambda e: e.matmul(
            kps[0][0:16, 0:1], c_s128[:, :], cntp[:, 0:1], start=True, stop=True))
        wait("vector", TT_, dm)
        cstep("vector", lambda e: e.tensor_copy(dn16[:, :], kps[0][0:16, 0:1]))
        cstep("vector", lambda e: e.tensor_scalar(
            t16b[:, :], z16[:, :], EPS_W, None, OP.mult))
        cstep("vector", lambda e: e.tensor_tensor(dn16[:, :], dn16[:, :], t16b[:, :], OP.add))
        rcd_n = cstep("vector", lambda e: e.reciprocal(dn16[:, :], dn16[:, :]))

        nm_n = cstep("vector", lambda e: e.tensor_scalar(
            t16a[:, :], Mrun[:, :], -1.0, None, OP.mult))
        for src, src_n, dst in ((t16a, nm_n, nM128), (dn16, rcd_n, rd128)):
            wait("tensor", TV, TV.n)
            t_n = cstep("tensor", lambda e, src=src: e.transpose(
                kps[1][0:1, 0:16], src[:, :], c_idf[0:16, 0:16]))
            wait("vector", TT_, t_n)
            rw_n = cstep("vector", lambda e: e.tensor_copy(row16[:, :], kps[1][0:1, 0:16]))
            wait("tensor", TV, rw_n)
            b2 = cstep("tensor", lambda e: e.matmul(
                kps[1][:, 0:16], ones1[:, :], row16[:, :], start=True, stop=True))
            wait("vector", TT_, b2)
            cstep("vector", lambda e, dst=dst: e.tensor_copy(dst[:, :], kps[1][:, 0:16]))

        wait("gpsimd", TV, TV.n)
        wait("gpsimd", TS, TS.n)
        ki_n = cstep("gpsimd", lambda e: e.iota(
            sc1[:, :], [[0, 16], [1, 64]], base=0, channel_multiplier=0,
            allow_small_or_imprecise_dtypes=True))
        wait("vector", TG, ki_n)
        cstep("vector", lambda e: e.tensor_scalar(
            sc1[:, :], sc1[:, :], -1.0, 64.0, OP.mult, OP.add))
        key_n = cstep("vector", lambda e: e.tensor_tensor(sc1[:, :], sc1[:, :], selb, OP.mult))
        wait("sync", TV, key_n)
        keyw_n = dstep("sync", S("keyw"), lambda e: e.dma_start(
            out=keyd.rearrange("e (s t) -> (e s) t", s=8), in_=sc1[:, :]))

        wait("gpsimd", S("set"), S("set").n)
        wait("gpsimd", S("keyw"), keyw_n)
        cstep("gpsimd", lambda e: e.iota(iotaS[:, :], [[1, 256]], base=0,
                                         channel_multiplier=0,
                                         allow_small_or_imprecise_dtypes=True))
        wait("vector", TG, TG.n)
        ids_of = {}
        tokv_of = {}
        idsd_flat = idsd.rearrange("e t -> (e t)").unsqueeze(1)
        for e_ in range(E):
            wait("sync", S("keyw"), keyw_n)
            if e_ > 0:
                wait("sync", TV, ids_of[e_ - 1])
            kl = dstep("sync", S("keyl"), lambda e, e_=e_: e.dma_start(
                out=keyt[:, :], in_=keyd[e_].rearrange("(p c) -> p c", c=64)))
            wait("vector", S("keyl"), kl)
            wait("vector", S("idsw"), S("idsw").n)
            np_n = cstep("vector", lambda e: e.tensor_scalar(
                keytab[:, :], keyt[:, :], 0.0, 0.0, OP.is_gt, OP.add,
                accum_out=cntp[:, 0:1]))
            for it in range(8):
                cstep("vector", lambda e, it=it: e.max(
                    out=keytab[:, it * 8:(it + 1) * 8], in_=keyt[:, :]))
                cstep("vector", lambda e, it=it: e.match_replace(
                    out=keyt[:, :], in_to_replace=keytab[:, it * 8:(it + 1) * 8],
                    in_values=keyt[:, :], imm_value=-1.0))
            ids_n = cstep("vector", lambda e: e.tensor_scalar(
                idst[:, :], keytab[:, :], pb64b[:, 0:1], -1.0, OP.subtract, OP.mult))
            ids_of[e_] = ids_n
            wait("tensor", TV, np_n)
            o_n = cstep("tensor", lambda e: e.matmul(
                kps[0][:, 0:1], c_ltri[:, :], cntp[:, 0:1], start=True, stop=True))
            wait("vector", TT_, o_n)
            cstep("vector", lambda e: e.tensor_copy(r3[:, 0:1], kps[0][:, 0:1]))
            a_n = cstep("vector", lambda e: e.tensor_scalar(
                r4[:, 0:1], r3[:, 0:1], pb64[:, 0:1], -1.0, OP.subtract, OP.mult))
            wait("tensor", TV, a_n)
            s_n = cstep("tensor", lambda e: e.matmul(
                kps[1][:, 0:1], c_shift[:, :], r4[:, 0:1], start=True, stop=True))
            wait("vector", TT_, s_n)
            cstep("vector", lambda e: e.tensor_copy(r2[:, 0:1], kps[1][:, 0:1]))
            d2_n = cstep("vector", lambda e: e.tensor_tensor(
                r4[:, 0:1], r4[:, 0:1], r2[:, 0:1], OP.subtract))
            i_n = cstep("vector", lambda e: e.tensor_scalar(
                Itile[:, :], iotaS[:, :], r3[:, 0:1], None, OP.is_ge))
            for sc_ in range(2):
                wait("tensor", TV, TV.n)
                f_n = cstep("tensor", lambda e, sc_=sc_: e.matmul(
                    kps[0][:, 0:1], Itile[:, sc_ * 128:(sc_ + 1) * 128], r4[:, 0:1],
                    start=True, stop=True))
                wait("vector", TT_, f_n)
                cstep("vector", lambda e: e.tensor_copy(r2[:, 0:1], kps[0][:, 0:1]))
                cstep("vector", lambda e: e.tensor_tensor(
                    r2[:, 0:1], r2[:, 0:1], iotaP[:, 0:1], OP.add))
                cstep("vector", lambda e, e_=e_, sc_=sc_: e.tensor_scalar(
                    r2[:, 0:1], r2[:, 0:1], float(e_ * T + sc_ * 128), None, OP.add))
                cv_n = cstep("vector", lambda e, e_=e_, sc_=sc_: e.tensor_copy(
                    flatu[:, e_, sc_:sc_ + 1], r2[:, 0:1]))
                if sc_ == 0:
                    wait("gpsimd", TV, ids_n)
                    dstep("gpsimd", S("idsw"), lambda e, e_=e_: e.dma_start(
                        out=idsd[e_].rearrange("(p r) -> p r", r=64), in_=idst[:, :]))
                wait("gpsimd", TV, cv_n)
                wait("gpsimd", S("idsw"), S("idsw").n)
                dstep("gpsimd", S("res"), lambda e, e_=e_, sc_=sc_: e.indirect_dma_start(
                    toks[:, e_, sc_:sc_ + 1], None, idsd_flat,
                    IndirectOffsetOnAxis(flatu[:, e_, sc_:sc_ + 1], 0)))
            wait("vector", S("res"), S("res").n)
            tokv_of[e_] = cstep("vector", lambda e, e_=e_: e.tensor_copy(
                toku[:, e_, :], toks[:, e_, :]))
        tok_n = tokv_of[E - 1]

        if upto == "topk":
            wait("sync", TV, tok_n)
            dstep("sync", S("dtk"), lambda e: e.dma_start(
                out=dbg["tok"], in_=toks[:, :, :].rearrange("p a b -> p (a b)")))
            for i, src in enumerate((mid16, n116, q16, z16, dn16, Mrun, mrun, lo16)):
                cstep("vector", lambda e, i=i, src=src: e.tensor_copy(
                    sc1[0:16, i:i + 1], src[:, :]))
            thd = TV.n
            wait("sync", TV, thd)
            dstep("sync", S("dtk"), lambda e: e.dma_start(
                out=dbg["th"], in_=sc1[0:16, 0:8]))
            for sname in ("dbgx0", "dbgx1", "acw0", "acw1", "gsw0", "gsw1",
                          "rtw", "rlw", "dbgr", "dtk"):
                wait("sync", S(sname))
            _emit(nc, blk, streams)
            return nc, dbg

        wait("gpsimd", S("gsw0"), S("gsw0").n)
        wait("gpsimd", S("gsw1"), S("gsw1").n)
        wait("gpsimd", S("rtw"), S("rtw").n)
        wait("gpsimd", TV, tokv_of[0])

        eit_of = {}
        eiw_of = {}
        ei_used = {}
        h2_used = {}
        silu3_of = {}
        h2v_of = {}
        numer_of = {}
        wt_of = {}
        eo7_of = {}
        rg_of, rlgg_of = {}, {}
        scat_prev = 0

        eiTs = (xm[:, 1, :, :], xm[:, 0, :, :])
        land = ybuf[:, :, :].rearrange("p a b -> p (a b)").bitcast(BF16)
        eop = None
        eov_last = {}

        for sc_ in range(2):
            rg_of[(0, sc_)] = dstep("gpsimd", S("rg0"),
                                    lambda e, sc_=sc_: e.indirect_dma_start(
                land[:, sc_ * 1024:(sc_ + 1) * 1024], None, gsrc,
                IndirectOffsetOnAxis(toku[:, 0, sc_:sc_ + 1], 0)))
            rlgg_of[(0, sc_)] = dstep("gpsimd", S("rlgg"),
                                      lambda e, sc_=sc_: e.indirect_dma_start(
                rlrow[:, sc_, :], None, rlg,
                IndirectOffsetOnAxis(toku[:, 0, sc_:sc_ + 1], 0)))

        for e_ in range(E):
            par3 = e_ % 2
            eiTc = eiTs[par3]
            if e_ + 1 < E:
                np1 = (e_ + 1) % 2
                wait("gpsimd", TV, tokv_of[e_ + 1])
                if e_ >= 1:
                    wait("gpsimd", S("eiw%d" % np1), eiw_of.get(e_ - 1, 0))
                    wait("gpsimd", TS, numer_of[e_ - 1])
                for sc_ in range(2):
                    rg_of[(e_ + 1, sc_)] = dstep(
                        "gpsimd", S("rg%d" % np1),
                        lambda e, e_=e_, sc_=sc_, np1=np1: e.indirect_dma_start(
                            land[:, np1 * 2048 + sc_ * 1024:
                                 np1 * 2048 + (sc_ + 1) * 1024], None, gsrc,
                            IndirectOffsetOnAxis(toku[:, e_ + 1, sc_:sc_ + 1], 0)))
                    rlgg_of[(e_ + 1, sc_)] = dstep(
                        "gpsimd", S("rlgg"),
                        lambda e, e_=e_, sc_=sc_, np1=np1: e.indirect_dma_start(
                            rlrow[:, np1 * 2 + sc_, :], None, rlg,
                            IndirectOffsetOnAxis(toku[:, e_ + 1, sc_:sc_ + 1], 0)))

            wait("sync", S("rg%d" % par3), rg_of[(e_, 1)])
            if e_ >= 2:
                wait("sync", S("eit%d" % par3), eit_of[e_ - 2])
            for sc_ in range(2):
                eiw_of[e_] = dstep("sync", S("eiw%d" % par3),
                                   lambda e, sc_=sc_, par3=par3: e.dma_start(
                    out=eisc[par3, sc_ * 128:(sc_ + 1) * 128, :].rearrange(
                        "(a p) c -> p a c", p=128),
                    in_=land[:, par3 * 2048 + sc_ * 1024:
                             par3 * 2048 + (sc_ + 1) * 1024].rearrange(
                        "p (a c) -> p a c", c=C)))
            wait("sync", S("eiw%d" % par3), eiw_of[e_])
            if e_ >= 2:
                wait("sync", TT_, ei_used[e_ - 2])
            for ct in range(CT):
                eit_of[e_] = dstep("sync", S("eit%d" % par3),
                                   lambda e, ct=ct, par3=par3, eiTc=eiTc: e.dma_start(
                    out=eiTc[:, ct, 0:SUB],
                    in_=eisc[par3, :, ct * 128:(ct + 1) * 128], transpose=True))

            wait("scalar", S("rlgg"), rlgg_of[(e_, 1)])
            wait("scalar", TV, tokv_of[e_])
            if e_ > 1:
                wait("scalar", TV, eov_last[e_ - 2])
            for sc_ in range(2):
                nu_n = cstep("scalar", lambda e, e_=e_, sc_=sc_: e.activation(
                    wtv[:, (e_ % 2) * 2 + sc_:(e_ % 2) * 2 + sc_ + 1],
                    rlrow[:, (e_ % 2) * 2 + sc_, e_:e_ + 1], AF.Exp,
                    bias=nM128[:, e_:e_ + 1]))
            numer_of[e_] = nu_n
            wait("vector", TS, nu_n)
            wt_of[e_] = cstep("vector", lambda e, e_=e_: e.tensor_tensor(
                wtv[:, (e_ % 2) * 2:(e_ % 2) * 2 + 2],
                wtv[:, (e_ % 2) * 2:(e_ % 2) * 2 + 2],
                rd128[:, e_:e_ + 1].to_broadcast([128, 2]), OP.mult))

            eop = ((wps[0], 0, 0), (wps[1], 0, 1), (rps, 1, 0), (lps, 1, 1))

            def eo_mms(ft):
                wait("tensor", TV, h2v_of[(e_, ft)])
                if ft == 0:
                    wait("tensor", S("w3l"), w3l1_of[e_])
                    if e_ > 0:
                        wait("tensor", TV, eov_last[e_ - 1])
                if ft == 8:
                    wait("tensor", S("w3l"), w3l_of[e_])
                for ps, sc_, dc in eop:
                    cstep("tensor", lambda e, ps=ps, sc_=sc_, dc=dc, ft=ft: e.matmul(
                        ps[:, :],
                        h2b[:, ft, sc_ * 128:(sc_ + 1) * 128],
                        w3a[ft // 8][:, ft % 8, dc * 512:(dc + 1) * 512],
                        start=(ft == 0), stop=(ft == 15)))
                if ft == 7:
                    eo7_of[e_] = TT_.n

            wait("tensor", S("eit%d" % par3), eit_of[e_])
            for fb in range(4):
                par = fb % 2
                wait("tensor", S("w1l0" if par == 0 else "w1l1"), w1l_of[(e_, fb)])
                wait("tensor", S("w2l0" if par == 0 else "w2l1"), w2l_of[(e_, fb)])
                for fl in range(4):
                    ft = fb * 4 + fl
                    pk = (e_, ft - 2) if ft >= 2 else (e_ - 1, ft + 14)
                    if pk in silu3_of:
                        wait("tensor", TS, silu3_of[pk])
                    for cc in range(CT):
                        gmm_n = cstep("tensor", lambda e, par=par, fl=fl, ft=ft, cc=cc,
                                      eiTc=eiTc: e.matmul(
                            kps[ft % 2][:, 0:SUB],
                            w1a[:, cc, par * 512 + fl * 128:par * 512 + (fl + 1) * 128],
                            eiTc[:, cc, 0:SUB],
                            start=(cc == 0), stop=(cc == CT - 1)))
                    if pk in h2v_of:
                        wait("tensor", TV, h2v_of[pk])
                    for cc in range(CT):
                        vmm_n = cstep("tensor", lambda e, par=par, fl=fl, ft=ft, cc=cc,
                                      eiTc=eiTc: e.matmul(
                            vps[ft % 2][:, 0:SUB],
                            w2a[:, cc, par * 512 + fl * 128:par * 512 + (fl + 1) * 128],
                            eiTc[:, cc, 0:SUB],
                            start=(cc == 0), stop=(cc == CT - 1)))
                    if ft > 0:
                        eo_mms(ft - 1)
                    wait("scalar", TT_, gmm_n)
                    silu3_of[(e_, ft)] = cstep("scalar", lambda e, ft=ft: e.activation(
                        h1g[:, ft * SUB:(ft + 1) * SUB], kps[ft % 2][:, 0:SUB], AF.Silu))
                    wait("vector", TT_, vmm_n)
                    wait("vector", TS, silu3_of[(e_, ft)])
                    h2v_of[(e_, ft)] = cstep("vector", lambda e, ft=ft: e.tensor_tensor(
                        h2b[:, ft, :], h1g[:, ft * SUB:(ft + 1) * SUB],
                        vps[ft % 2][:, 0:SUB], OP.mult))
                g_done[(e_, fb)] = TT_.n
                if fb + 2 < 4:
                    wload(e_, fb + 2)
            ei_used[e_] = g_done[(e_, 3)]
            eo_mms(15)
            h2_used[e_] = TT_.n
            eo_done = TT_.n

            if e_ + 1 < E:
                wload(e_ + 1, 0)
                wload(e_ + 1, 1)
                wait("gpsimd", TT_, eo7_of[e_])
                w3l1_of[e_ + 1] = dstep("gpsimd", S("w3l"),
                                        lambda e, e_=e_: e.dma_start(
                    out=wkT[:, :, :],
                    in_=w3d[e_ + 1, 0:1024, :].rearrange("(a p) d -> p a d", p=128)))
                wait("gpsimd", TT_, h2_used[e_])
                w3l_of[e_ + 1] = dstep("gpsimd", S("w3l"),
                                       lambda e, e_=e_: e.dma_start(
                    out=wvT[:, :, :],
                    in_=w3d[e_ + 1, 1024:2048, :].rearrange("(a p) d -> p a d", p=128)))

            wait("vector", TT_, eo_done)
            wait("vector", TV, wt_of[e_])
            if e_ > 0:
                wait("vector", S("scat"), scat_prev)
            for ps, sc_, dc in eop:
                eov_last[e_] = cstep(
                    "vector", lambda e, ps=ps, sc_=sc_, dc=dc, e_=e_: e.tensor_scalar(
                        eob[:, sc_ * 2048 + dc * 512:sc_ * 2048 + (dc + 1) * 512],
                        ps[:, :], wtv[:, (e_ % 2) * 2 + sc_:(e_ % 2) * 2 + sc_ + 1],
                        None, OP.mult))

            scat_prev = S("scat").n
            for sc_ in range(2):
                wait("gpsimd", TV, eov_last[e_])
                wait("gpsimd", S("acw0"), S("acw0").n)
                wait("gpsimd", S("acw1"), S("acw1").n)
                wait("gpsimd", S("scat"), S("scat").n)
                dstep("gpsimd", S("scat"), lambda e, e_=e_, sc_=sc_: e.indirect_dma_start(
                    accum, IndirectOffsetOnAxis(toku[:, e_, sc_:sc_ + 1], 0),
                    eob[:, sc_ * 2048:sc_ * 2048 + 1024], None,
                    compute_op=OP.add, oob_is_err=True))
            scat_prev = S("scat").n

        wait("scalar", S("scat"), S("scat").n)
        wait("scalar", TV, TV.n)
        wait("sync", S("scat"), S("scat").n)
        wait("sync", TV, TV.n)
        stg = [kvb[:, :, :].rearrange("p a b -> p (a b)").rearrange(
                   "p (a b) -> p a b", a=4),
               kcb[:, :, :].rearrange("p a b -> p (a b)").rearrange(
                   "p (a b) -> p a b", a=4)]
        ngr = T // 512
        gld_of, gst_of = {}, {}
        wait("gpsimd", TV, TV.n)
        wait("gpsimd", S("scat"), S("scat").n)
        for g in range(ngr):
            pr = g % 2
            buf = stg[pr]
            cbase = pr * 4
            if g >= 2:
                wait("sync", S("outw%d" % pr), gst_of[g - 2])
                wait("gpsimd", S("outw%d" % pr), gst_of[g - 2])
            tl = 0
            for i in range(4):
                rt = g * 4 + i
                tl = dstep("sync" if i < 2 else "gpsimd", S("tld"),
                           lambda e, rt=rt, buf=buf, i=i:
                           e.dma_start(out=buf[:, i, :],
                                       in_=accum[rt * 128:(rt + 1) * 128, :]))
            wait("scalar", S("tld"), tl)
            wait("vector", S("tld"), tl)

            for i in range(4):
                cstep("scalar", lambda e, buf=buf, i=i, cbase=cbase: e.activation(
                    sc1[:, :], buf[:, i, :], AF.Square,
                    accum_out=lsq[:, cbase + i:cbase + i + 1]))
                lns_n = cstep("vector", lambda e, buf=buf, i=i, cbase=cbase:
                              e.tensor_scalar(
                    buf[:, i, :], buf[:, i, :], 1.0, 0.0, OP.mult, OP.add,
                    accum_out=lmn[:, cbase + i:cbase + i + 1]))
            wait("vector", TS, TS.n)
            cb4 = slice(cbase, cbase + 4)
            cstep("vector", lambda e, cb4=cb4: e.tensor_scalar(
                lrs[:, cb4], lsq[:, cb4], 1.0 / C, EPS_LN, OP.mult, OP.add))
            cstep("vector", lambda e, cb4=cb4: e.tensor_scalar(
                lmn[:, cb4], lmn[:, cb4], 1.0 / C, None, OP.mult))
            cstep("vector", lambda e, cb4=cb4: e.tensor_tensor(
                nM128[:, cb4], lmn[:, cb4], lmn[:, cb4], OP.mult))
            var_n = cstep("vector", lambda e, cb4=cb4: e.tensor_tensor(
                lrs[:, cb4], lrs[:, cb4], nM128[:, cb4], OP.subtract))
            wait("scalar", TV, var_n)
            sq_n = cstep("scalar", lambda e, cb4=cb4: e.activation(
                lrs[:, cb4], lrs[:, cb4], AF.Sqrt))
            wait("vector", TS, sq_n)
            cstep("vector", lambda e, cb4=cb4: e.reciprocal(
                lrs[:, cb4], lrs[:, cb4]))
            cstep("vector", lambda e, cb4=cb4: e.tensor_scalar(
                nM128[:, cb4], lmn[:, cb4], -1.0, None, OP.mult))
            for i in range(4):
                no_n = cstep("vector", lambda e, buf=buf, i=i, cbase=cbase:
                             e.tensor_scalar(
                    buf[:, i, :], buf[:, i, :],
                    nM128[:, cbase + i:cbase + i + 1],
                    lrs[:, cbase + i:cbase + i + 1], OP.add, OP.mult))
            wait("scalar", TV, no_n)
            for i in range(4):
                rt = g * 4 + i
                gst_of[g] = dstep("scalar", S("outw%d" % pr),
                                  lambda e, rt=rt, buf=buf, i=i: e.dma_start(
                    out=out[rt * 128:(rt + 1) * 128, :], in_=buf[:, i, :]))

        wait("sync", S("outw0"), S("outw0").n)
        wait("sync", S("outw1"), S("outw1").n)
        _emit(nc, blk, streams)
        return nc, dbg


def _emit(nc, blk, streams):
    for eng in ("sync", "scalar", "vector", "tensor", "gpsimd"):
        ops = streams[eng]

        def make(ops):
            def run(e):
                for fn in ops:
                    fn(e)
            return run
        getattr(blk, eng)(make(ops))


_NC_CACHE = {}


def kernel(**inputs):
    cs = _consts()
    if "full" not in _NC_CACHE:
        _NC_CACHE["full"] = build_program(upto="full")
    nc, _ = _NC_CACHE["full"]

    def hostv(a):
        return np.ascontiguousarray(np.asarray(a), dtype=np.float32)

    xq = hostv(inputs["L0_attn_xq"])
    base = dict(
        Wk=hostv(inputs["L0_attn_Wk"]), Wv=hostv(inputs["L0_attn_Wv"]),
        Wr=hostv(inputs["L0_attn_Wr"]), Wo=hostv(inputs["L0_attn_Wo"]),
        tmk=hostv(inputs["L0_attn_time_mix_k"]).reshape(-1),
        tmv=hostv(inputs["L0_attn_time_mix_v"]).reshape(-1),
        tmr=hostv(inputs["L0_attn_time_mix_r"]).reshape(-1),
        td=hostv(inputs["L0_attn_time_decay"]).reshape(-1),
        tfi=hostv(inputs["L0_attn_time_first"]).reshape(-1),
        rw1=hostv(inputs["L0_ffn_router_w1"]),
        rw2=hostv(inputs["L0_ffn_router_w2"]),
        w1=hostv(inputs["L0_ffn_w1"]), w2=hostv(inputs["L0_ffn_w2"]),
        w3=hostv(inputs["L0_ffn_w3"]),
    )
    for k, v in cs.items():
        base["cin_" + k] = v
    in_maps = []
    for c in range(8):
        m = dict(base)
        m["xq"] = xq[c // 2]
        in_maps.append(m)
    res = run_bass_kernel_spmd(nc, in_maps, core_ids=list(range(8)))
    outp = np.empty((B, T, C), np.float32)
    for b in range(B):
        outp[b] = res.results[2 * b]["out"]
    return outp
